# version 06
# baseline (speedup 1.0000x reference)
import numpy as np
import ml_dtypes
import concourse.bass as bass
import concourse.mybir as mybir
from concourse.ap import AP
from concourse.bass_utils import run_bass_kernel_spmd
from contextlib import ExitStack

F32 = mybir.dt.float32
BF16 = mybir.dt.bfloat16
AF = mybir.ActivationFunctionType
ALU = mybir.AluOpType
AX = mybir.AxisListType

BL, S, D = 2, 2048, 1024
NH, DH = 16, 64
DI, NST, DTR = 2048, 16, 64
E, F, R = 8, 4096, 16
PAD = -30000.0
EPS = 1e-5

N_SUBLAYERS = 4
DEBUG = 0

ENGINES = ("sync", "tensor", "vector", "scalar", "gpsimd")

ARN_COLS = 34048
AT_LNX = 0
AT_Q = 16384
AT_K = 32768
AT_V = 49152
FF_XA1 = 16384
FF_XA3 = 18432
FF_XM = 20480
FF_M1X = 28672
FF_W1B = 36864
FF_WACT = 40960
FF_A1 = 57344
FF_ZW = 59392
FF_A2C = 60416
FF_B2C = 64512
FF_A1CT = 65536
FF_A3CT = 66560
FF_GATE = 67584


class Prog:
    def __init__(self, nc, ctx):
        self.nc, self.ctx = nc, ctx
        self.q = {e: [] for e in ENGINES}
        self.sems, self.cnt = {}, {}
        self.waited = {e: {} for e in ENGINES}

    def sem(self, name):
        if name not in self.sems:
            self.sems[name] = self.ctx.enter_context(self.nc.semaphore(name))
            self.cnt[name] = 0
        return self.sems[name]

    def wait(self, eng, sname, val=None):
        if val is None:
            val = self.cnt.get(sname, 0)
        if val <= 0:
            return
        h = self.sem(sname)
        if self.waited[eng].get(sname, -1) >= val:
            return
        self.waited[eng][sname] = val
        self.q[eng].append(lambda e, h=h, v=val: e.wait_ge(h, v))

    def op(self, eng, fn, inc=None, amt=1):
        if inc is None:
            self.q[eng].append(fn)
            return None
        h = self.sem(inc)
        self.cnt[inc] += amt
        self.q[eng].append(lambda e, fn=fn, h=h, a=amt: fn(e).then_inc(h, a))
        return self.cnt[inc]

    def c(self, sname):
        return self.cnt.get(sname, 0)

    def barrier(self, engs=ENGINES, sems=("T", "V", "S", "ld", "ldg", "st", "lgA", "lgB", "lgW", "PV")):
        for eng in engs:
            for s in sems:
                self.wait(eng, s)

    def replay(self, block):
        for eng in ENGINES:
            ops = self.q[eng]
            if not ops:
                continue

            def make(ops):
                def runner(e):
                    for fn in ops:
                        fn(e)
                return runner
            getattr(block, eng)(make(ops))


def build_program():
    nc = bass.Bass("TRN2", target_bir_lowering=False, debug=False)

    def I(name, shape, dt=F32):
        return nc.dram_tensor(name, shape, dt, kind="ExternalInput").ap()

    xT = I("xT", [BL, D, S])
    rpe = I("rpe", [NH, S])
    qkvT = I("qkvT", [D, 3 * NH * DH])
    outwT = I("outwT", [NH * DH, D])
    inWB = I("inWB", [2, 32, 128, 1024])
    convWB = I("convWB", [2, 128, 64])
    convBv = I("convBv", [2, 128, 16])
    xpWB = I("xpWB", [2, 128, 2048])
    dtWT = I("dtWT", [2, DTR, DI])
    dtBv = I("dtBv", [2, 128, 16])
    AlogB = I("AlogB", [2, 128, 256])
    Dpv = I("Dpv", [2, 128, 16])
    moWB = I("moWB", [2, 8, 128, 2048])
    gateB = I("gateB", [128, 64])
    qkvB = I("qkvB", [24, 128, 1024])
    WgB = I("WgB", [32, 128, 1024])
    WuB = I("WuB", [32, 128, 1024])
    WdB = I("WdB", [8, 128, 4096])
    A1cB = I("A1cB", [128, 1024])
    A3cB = I("A3cB", [128, 1024])
    B1c = I("B1c", [128, F])
    B3c = I("B3c", [128, F])
    A2cB = I("A2cB", [128, 4096])
    B2c = I("B2c", [128, D])
    Jrev = I("Jrev", [128, 128])
    Ident = I("Ident", [128, 128])
    Itwo = I("Itwo", [128, 128])
    onesc = I("onesc", [128, 1])
    ones1r = I("ones1r", [1, 128])
    exp16 = I("exp16", [E, 128])
    cextra = I("cextra", [128, 8])

    out_xT = nc.dram_tensor("out_xT", [BL, D, S], F32, kind="ExternalOutput").ap()
    dbg = {}
    if DEBUG == 1:
        for nm, cols in [("lnx", 16384), ("q", 16384), ("k", 16384),
                         ("v", 16896), ("mix", 16384)]:
            dbg[nm] = nc.dram_tensor("dbg_" + nm, [128, cols], BF16,
                                     kind="ExternalOutput").ap()
    if DEBUG == 2:
        dbg["ffn1"] = nc.dram_tensor("dbg_ffn1", [128, 1152], F32,
                                     kind="ExternalOutput").ap()
        for nm, cols in [("ffn2", 16384), ("ffn3", 8192), ("ffn4", 19456)]:
            dbg[nm] = nc.dram_tensor("dbg_" + nm, [128, cols], BF16,
                                     kind="ExternalOutput").ap()
    x_cur = nc.dram_tensor("x_cur", [BL, D, S], F32, kind="Internal").ap()
    g_dram = nc.dram_tensor("g_dram", [NH, 4095], F32, kind="Internal").ap()
    mm_bc = nc.dram_tensor("mm_bc", [2, 16, 2048], BF16, kind="Internal").ap()
    mm_z = nc.dram_tensor("mm_z", [16, 128, 2048], BF16, kind="Internal").ap()
    mm_yp = nc.dram_tensor("mm_yp", [D, S], F32, kind="Internal").ap()

    ctx = ExitStack()
    with ctx:
        def SB(name, p, cols, dt=F32):
            return ctx.enter_context(nc.sbuf_tensor(name, [p, cols], dt))

        ARN = SB("ARN", 128, ARN_COLS)
        sc1 = SB("sc1", 128, 2048)
        sc2 = SB("sc2", 128, 2048)
        sc3 = SB("sc3", 128, 2048)
        sc4 = SB("sc4", 128, 2048)
        xst = SB("xst", 128, 4096)
        wsl = SB("wsl", 128, 4096, BF16)
        wsl2 = SB("wsl2", 128, 4096, BF16)
        wsm = SB("wsm", 128, 1024, BF16)
        bprob = SB("bprob", 128, 1024, BF16)
        btr = SB("btr", 128, 1024)
        cJ = SB("cJ", 128, 128)
        cI = SB("cI", 128, 128)
        cI2 = SB("cI2", 128, 128)
        cOc = SB("cOc", 128, 1)
        cX = SB("cX", 128, 8)
        cOr = ctx.enter_context(nc.sbuf_tensor("cOr", [1, 128], F32))
        cE16 = ctx.enter_context(nc.sbuf_tensor("cE16", [E, 128], F32))
        cGate = SB("cGate", 128, 64)

        ARNB_t = ARN.ap().bitcast(BF16).tensor
        ARN_t = ARN.ap().tensor
        SC2B_t = sc2.ap().bitcast(BF16).tensor

        PS = [ctx.enter_context(nc.psum_tensor(f"pp{i}", [128, 512], F32)) for i in range(8)]

        block = ctx.enter_context(nc.Block())
        P = Prog(nc, ctx)
        LD, T, V, SS = "ld", "T", "V", "S"
        LG, STQ = "ldg", "st"

        def dma(dst, src):
            return P.op("sync", lambda e, d=dst, s=src: e.dma_start(out=d, in_=s), inc=LD, amt=16)

        def dma_st(dst, src):
            return P.op("sync", lambda e, d=dst, s=src: e.dma_start(out=d, in_=s), inc=STQ, amt=16)

        def dma_cast(dst, src, semn=None):
            return P.op("gpsimd", lambda e, d=dst, s=src: e.dma_start(out=d, in_=s),
                        inc=(semn or LG), amt=16)

        def sl(tens, off, p, w, pstride):
            return AP(tens.ap().tensor, off, [[pstride, p], [1, w]])

        def bv(off, dims, p=128, base=0):
            return AP(ARNB_t, off + base * 2 * ARN_COLS, [[2 * ARN_COLS, p]] + dims)

        def fv(off, dims, p=128, base=0):
            return AP(ARN_t, off + base * ARN_COLS, [[ARN_COLS, p]] + dims)

        dma(cJ.ap(), Jrev)
        dma(cI.ap(), Ident)
        dma(cI2.ap(), Itwo)
        dma_cast(sl(wsm, 0, 128, 128, 1024), Ident)
        dma(cOc.ap(), onesc)
        dma(cX.ap(), cextra)
        dma(cOr.ap(), ones1r)
        c_rpe = dma(sl(sc1, 0, NH, S, 2048), rpe)
        dma(cE16.ap(), exp16)
        c_gate = dma(cGate.ap(), gateB)
        c_xcur = dma(x_cur, xT)
        P.wait("tensor", LD, c_gate)
        for c in range(8):
            P.op("tensor", lambda e, c=c: e.matmul(
                PS[4][0:8, 0:1], sl(cGate, c * 8, 128, 8, 64), cOc[:, :],
                start=(c == 0), stop=(c == 7)), inc=(T if c == 7 else None))
        P.wait("scalar", T, P.c(T))
        P.wait("scalar", LD, P.c(LD))
        P.op("scalar", lambda e: e.activation(
            out=cX[0:8, 4:5], in_=PS[4][0:8, 0:1],
            func=AF.Copy, scale=-1.0), inc=SS)

        P.wait("vector", LD, c_rpe)
        P.op("vector", lambda e: e.memset(sl(sc2, 0, NH, 2047, 2048), PAD))
        P.op("vector", lambda e: e.tensor_tensor_scan(
            out=sl(sc3, 0, NH, S, 2048),
            data0=AP(cOc.ap().tensor, 0, [[1, NH], [0, S]]),
            data1=sl(sc1, 0, NH, S, 2048),
            initial=0.0, op0=ALU.mult, op1=ALU.add), inc=V)
        P.wait("sync", V, P.c(V))
        dma(AP(g_dram.tensor, 0, [[4095, NH], [1, 2047]]), sl(sc2, 0, NH, 2047, 2048))
        dma(AP(g_dram.tensor, 2047, [[4095, NH], [1, 2048]]), sl(sc3, 0, NH, S, 2048))

        def lnx_sl(chunk, t0, w, lnx_off=AT_LNX):
            return AP(ARNB_t, lnx_off + chunk * 2048 + t0, [[2 * ARN_COLS, 128], [1, w]])

        def emit_ln(b, lnx_off=AT_LNX, reverse=False, gate=False):
            P.barrier()
            gmark = 0
            gv_prev = 0
            for tb in range(4):
                t0 = tb * 512
                lds = []
                for c in range(8):
                    lds.append(dma(sl(xst, c * 512, 128, 512, 4096),
                                   AP(x_cur.tensor, (b * D + c * 128) * S + t0,
                                      [[S, 128], [1, 512]])))
                sq_mm_T = {}
                for c in range(8):
                    P.wait("scalar", LD, lds[c])
                    if c >= 4:
                        P.wait("scalar", T, sq_mm_T[c - 4])
                    P.op("scalar", lambda e, c=c: e.activation(
                        out=sl(sc1, (c % 4) * 512, 128, 512, 2048),
                        in_=sl(xst, c * 512, 128, 512, 4096),
                        func=AF.Square), inc=SS)
                    sq_c = P.c(SS)
                    P.wait("tensor", LD, lds[c])
                    P.op("tensor", lambda e, c=c: e.matmul(
                        PS[0][0:1, :], cOc[:, :], sl(xst, c * 512, 128, 512, 4096),
                        start=(c == 0), stop=(c == 7)))
                    if gate:
                        if c == 0 and tb > 0:
                            P.wait("tensor", SS, gmark)
                        P.op("tensor", lambda e, c=c: e.matmul(
                            PS[5][0:8, :], sl(cGate, c * 8, 128, 8, 64),
                            sl(xst, c * 512, 128, 512, 4096),
                            start=(c == 0), stop=(c == 7)))
                    P.wait("tensor", SS, sq_c)
                    sq_mm_T[c] = P.op("tensor", lambda e, c=c: e.matmul(
                        PS[1][0:1, :], cOc[:, :], sl(sc1, (c % 4) * 512, 128, 512, 2048),
                        start=(c == 0), stop=(c == 7)), inc=T)
                P.wait("scalar", T, P.c(T))
                if gate:
                    gmark = P.op("scalar", lambda e, t0=t0: e.activation(
                        out=sl(sc4, t0, 8, 512, 2048),
                        in_=PS[5][0:8, :], func=AF.Copy), inc=SS)
                P.op("scalar", lambda e: e.activation(
                    out=sl(sc2, 0, 1, 512, 2048), in_=PS[0][0:1, :],
                    func=AF.Copy, scale=cX[0:1, 1:2]))
                P.op("scalar", lambda e: e.activation(
                    out=sl(sc2, 512, 1, 512, 2048), in_=PS[1][0:1, :],
                    func=AF.Copy, scale=cX[0:1, 1:2]), inc=SS)
                P.wait("vector", SS, P.c(SS))
                P.op("vector", lambda e: e.tensor_mul(
                    out=sl(sc2, 1024, 1, 512, 2048),
                    in0=sl(sc2, 0, 1, 512, 2048), in1=sl(sc2, 0, 1, 512, 2048)))
                P.op("vector", lambda e: e.tensor_sub(
                    out=sl(sc2, 1024, 1, 512, 2048),
                    in0=sl(sc2, 512, 1, 512, 2048), in1=sl(sc2, 1024, 1, 512, 2048)), inc=V)
                P.wait("scalar", V, P.c(V))
                P.op("scalar", lambda e: e.activation(
                    out=sl(sc2, 1536, 1, 512, 2048), in_=sl(sc2, 1024, 1, 512, 2048),
                    func=AF.Ln, bias=cX[0:1, 0:1]))
                P.op("scalar", lambda e: e.activation(
                    out=sl(sc2, 1024, 1, 512, 2048), in_=sl(sc2, 1536, 1, 512, 2048),
                    func=AF.Exp, scale=cX[0:1, 2:3]), inc=SS)
                P.wait("vector", SS, P.c(SS))
                if gate:
                    P.wait("tensor", SS, P.c(SS))
                    if tb > 0:
                        P.wait("tensor", V, gv_prev)
                    P.op("tensor", lambda e: e.matmul(
                        PS[6][0:8, :], cOr[0:1, 0:8], sl(sc2, 0, 1, 512, 2048),
                        start=True, stop=True))
                    P.op("tensor", lambda e: e.matmul(
                        PS[7][0:8, :], cOr[0:1, 0:8], sl(sc2, 1024, 1, 512, 2048),
                        start=True, stop=True), inc=T)
                    P.wait("vector", T, P.c(T))
                    P.op("vector", lambda e, t0=t0: e.scalar_tensor_tensor(
                        AP(sc2.ap().tensor, 1536, [[2048, 8], [1, 512]]),
                        PS[6][0:8, :], cX[0:8, 4:5],
                        sl(sc4, t0, 8, 512, 2048), ALU.mult, ALU.add), inc=V)
                    P.wait("vector", V, P.c(V))
                    P.op("vector", lambda e, t0=t0: e.tensor_mul(
                        sl(sc4, t0, 8, 512, 2048),
                        AP(sc2.ap().tensor, 1536, [[2048, 8], [1, 512]]),
                        PS[7][0:8, :]), inc=V)
                    gv_prev = P.c(V)
                P.op("vector", lambda e: e.tensor_mul(
                    out=sl(sc2, 0, 1, 512, 2048),
                    in0=sl(sc2, 0, 1, 512, 2048), in1=sl(sc2, 1024, 1, 512, 2048)), inc=V)
                P.wait("tensor", V, P.c(V))
                P.op("tensor", lambda e: e.matmul(
                    PS[2][:, :], cOr[0:1, :], sl(sc2, 1024, 1, 512, 2048),
                    start=True, stop=True))
                P.op("tensor", lambda e: e.matmul(
                    PS[3][:, :], cOr[0:1, :], sl(sc2, 0, 1, 512, 2048),
                    start=True, stop=True), inc=T)
                P.wait("vector", T, P.c(T))
                for c in range(8):
                    P.op("vector", lambda e, c=c: e.tensor_mul(
                        out=sl(sc3, (c % 4) * 512, 128, 512, 2048),
                        in0=sl(xst, c * 512, 128, 512, 4096), in1=PS[2][:, :]))
                    if not reverse:
                        oap = AP(ARNB_t, lnx_off + c * 2048 + t0, [[2 * ARN_COLS, 128], [1, 512]])
                    else:
                        oap = AP(ARNB_t, lnx_off + c * 2048 + (S - 1 - t0),
                                 [[2 * ARN_COLS, 128], [-1, 512]])
                    P.op("vector", lambda e, c=c, oap=oap: e.tensor_sub(
                        out=oap,
                        in0=sl(sc3, (c % 4) * 512, 128, 512, 2048),
                        in1=PS[3][:, :]), inc=(V if c == 7 else None))
                P.wait("tensor", V, P.c(V))
                P.wait("scalar", V, P.c(V))
                if tb < 3:
                    P.wait("sync", V, P.c(V))
            return P.c(V)

        def dump(nm, off, cols):
            if not DEBUG or nm not in dbg:
                return
            P.barrier()
            dma_st(dbg[nm], AP(ARNB_t, off, [[2 * ARN_COLS, 128], [1, cols]]))
            P.wait("sync", STQ, P.c(STQ))

        def emit_attn(b):
            emit_ln(b, AT_LNX)
            ln_done = P.c(V)
            if b == 0:
                dump("lnx", AT_LNX, 16384)

            for kind in range(2):
                dstoff = AT_Q if kind == 0 else AT_K
                for p in range(8):
                    mcol = kind * 1024 + p * 128
                    slab = wsl if p % 2 == 0 else wsl2
                    sgn = "lgA" if p % 2 == 0 else "lgB"
                    cw = dma_cast(sl(slab, 0, 128, 1024, 4096),
                                  AP(qkvB.tensor, (kind * 8 + p) * 128 * 1024,
                                     [[1024, 128], [1, 1024]]), semn=sgn)
                    P.wait("tensor", sgn, cw)
                    P.wait("tensor", V, ln_done)
                    for sub in range(4):
                        bank = 4 + (sub % 2)
                        for c in range(8):
                            P.op("tensor", lambda e, c=c, sub=sub, slab=slab, bank=bank: e.matmul(
                                PS[bank][:, :], sl(slab, c * 128, 128, 128, 4096),
                                lnx_sl(c, sub * 512, 512),
                                start=(c == 0), stop=(c == 7)), inc=(T if c == 7 else None))
                        P.wait("scalar", T, P.c(T))
                        P.op("scalar", lambda e, p=p, sub=sub, dstoff=dstoff, kind=kind, bank=bank: e.activation(
                            out=bv(dstoff + p * 2048 + sub * 512, [[1, 512]]),
                            in_=PS[bank][:, :], func=AF.Copy,
                            scale=(cX[:, 3:4] if kind == 0 else 1.0)), inc=SS)
                        P.wait("tensor", SS, P.c(SS) - 1)
                    P.wait("tensor", SS, P.c(SS))
                    P.wait("gpsimd", T, P.c(T))

            if b == 0:
                dump("q", AT_Q, 16384)
                dump("k", AT_K, 16384)
            P.op("vector", lambda e: e.memset(
                AP(ARNB_t, AT_V + 64, [[2 * ARN_COLS, 128], [66, 256], [1, 1]]), 1.0), inc=V)
            P.wait("gpsimd", T)
            cw = None
            for c in range(8):
                dma_cast(sl(wsl, c * 512, 128, 512, 4096),
                         AP(qkvT.tensor, c * 128 * 3072 + 2048, [[3072, 128], [1, 512]]))
                cw = dma_cast(sl(wsl2, c * 512, 128, 512, 4096),
                              AP(qkvT.tensor, c * 128 * 3072 + 2048 + 512,
                                 [[3072, 128], [1, 512]]))
            P.wait("tensor", LG, cw)
            P.wait("tensor", V, P.c(V))
            for j in range(16):
                for half in range(2):
                    bank = 6 + half
                    slab = wsl if half == 0 else wsl2
                    for c in range(8):
                        P.op("tensor", lambda e, j=j, c=c, slab=slab, bank=bank: e.matmul(
                            PS[bank][:, :], lnx_sl(c, j * 128, 128),
                            sl(slab, c * 512, 128, 512, 4096),
                            start=(c == 0), stop=(c == 7)), inc=(T if c == 7 else None))
                P.wait("scalar", T, P.c(T))
                for half in range(2):
                    P.op("scalar", lambda e, j=j, half=half: e.activation(
                        out=AP(ARNB_t, AT_V + j * (16 * 66) + half * 8 * 66,
                               [[2 * ARN_COLS, 128], [66, 8], [1, 64]]),
                        in_=PS[6 + half][:, :], func=AF.Copy), inc=SS)
                P.wait("tensor", SS, P.c(SS))
            if b == 0:
                dump("v", AT_V, 16896)
            P.barrier(engs=("vector", "scalar"))

            last_qk_T = {}
            last_exp_S = {}
            last_pv_T = {}
            for n in range(16):
                p, hb = n // 2, (n % 2) * 64
                for sqb in range(4):
                    i0 = sqb * 512
                    nsk = sqb * 4 + 4
                    for jt in range(nsk):
                        j0 = jt * 128
                        buf = jt % 2
                        if buf in last_qk_T:
                            P.wait("sync", T, last_qk_T[buf])
                        c_t = dma(sl(btr, buf * 512, 128, 512, 1024),
                                  AP(g_dram.tensor, n * 4095 + 1920 - j0 + i0,
                                     [[1, 128], [1, 512]]))
                        P.wait("tensor", LD, c_t)
                        if buf in last_exp_S:
                            P.wait("tensor", SS, last_exp_S[buf])
                        P.op("tensor", lambda e, buf=buf: e.matmul(
                            PS[4 + buf][:, :], cJ[:, :], sl(btr, buf * 512, 128, 512, 1024),
                            start=True, stop=False))
                        last_qk_T[buf] = P.op("tensor", lambda e, hb=hb, p=p, j0=j0, i0=i0, buf=buf: e.matmul(
                            PS[4 + buf][:, :],
                            AP(ARNB_t, AT_K + hb * 2 * ARN_COLS + p * 2048 + j0,
                               [[2 * ARN_COLS, 64], [1, 128]]),
                            AP(ARNB_t, AT_Q + hb * 2 * ARN_COLS + p * 2048 + i0,
                               [[2 * ARN_COLS, 64], [1, 512]]),
                            start=False, stop=True), inc=T)
                        P.wait("scalar", T, last_qk_T[buf])
                        if buf in last_pv_T:
                            P.wait("scalar", T, last_pv_T[buf])
                        last_exp_S[buf] = P.op("scalar", lambda e, buf=buf: e.activation(
                            out=sl(bprob, buf * 512, 128, 512, 1024),
                            in_=PS[4 + buf][:, :], func=AF.Exp), inc=SS)
                        P.wait("tensor", SS, last_exp_S[buf])
                        last_pv_T[buf] = P.op("tensor", lambda e, n=n, jt=jt, buf=buf, nsk=nsk: e.matmul(
                            PS[2][0:65, :],
                            AP(ARNB_t, AT_V + jt * (16 * 66) + n * 66,
                               [[2 * ARN_COLS, 128], [1, 65]]),
                            sl(bprob, buf * 512, 128, 512, 1024),
                            start=(jt == 0), stop=(jt == nsk - 1)), inc=T)
                    P.wait("vector", T, P.c(T))
                    P.op("vector", lambda e: e.reciprocal(
                        out=sl(sc4, 0, 1, 512, 2048), in_=PS[2][64:65, :]), inc=V)
                    P.wait("tensor", V, P.c(V))
                    P.op("tensor", lambda e: e.matmul(
                        PS[3][0:64, :], cOr[0:1, 0:64], sl(sc4, 0, 1, 512, 2048),
                        start=True, stop=True), inc=T)
                    P.wait("scalar", T, P.c(T))
                    P.op("scalar", lambda e: e.activation(
                        out=sl(sc4, 512, 64, 512, 2048), in_=PS[3][0:64, :],
                        func=AF.Copy), inc=SS)
                    P.wait("vector", SS, P.c(SS))
                    P.op("vector", lambda e, hb=hb, p=p, i0=i0: e.tensor_mul(
                        out=AP(ARNB_t, AT_LNX + hb * 2 * ARN_COLS + p * 2048 + i0,
                               [[2 * ARN_COLS, 64], [1, 512]]),
                        in0=PS[2][0:64, :], in1=sl(sc4, 512, 64, 512, 2048)), inc=V)
                    P.wait("tensor", V, P.c(V))

            if b == 0:
                dump("mix", AT_LNX, 16384)
            P.wait("gpsimd", T)
            cw = None
            for k in range(8):
                slab, col = (wsl, (k % 4) * 1024) if k < 4 else (wsl2, (k % 4) * 1024)
                for hh in range(2):
                    cw = dma_cast(sl(slab, col + hh * 512, 128, 512, 4096),
                                  AP(outwT.tensor, k * 128 * D + hh * 512,
                                     [[D, 128], [1, 512]]))
            P.wait("tensor", LG, cw)
            for dtile in range(8):
                for sub in range(4):
                    t0 = sub * 512
                    bank = 4 + (sub % 2)
                    c_x = dma(sl(xst, (dtile % 2) * 2048 + sub * 512, 128, 512, 4096),
                              AP(x_cur.tensor, (b * D + dtile * 128) * S + t0,
                                 [[S, 128], [1, 512]]))
                    for k in range(8):
                        slab, col = (wsl, (k % 4) * 1024) if k < 4 else (wsl2, (k % 4) * 1024)
                        P.op("tensor", lambda e, k=k, dtile=dtile, t0=t0, slab=slab, col=col, bank=bank: e.matmul(
                            PS[bank][:, :],
                            sl(slab, col + dtile * 128, 128, 128, 4096),
                            AP(ARNB_t, AT_LNX + k * 2048 + t0, [[2 * ARN_COLS, 128], [1, 512]]),
                            start=(k == 0), stop=(k == 7)), inc=(T if k == 7 else None))
                    P.wait("vector", T, P.c(T))
                    P.wait("vector", LD, c_x)
                    P.op("vector", lambda e, dtile=dtile, sub=sub, bank=bank: e.tensor_add(
                        out=sl(sc3, (sub % 2) * 512, 128, 512, 2048),
                        in0=sl(xst, (dtile % 2) * 2048 + sub * 512, 128, 512, 4096),
                        in1=PS[bank][:, :]), inc=V)
                    P.wait("sync", V, P.c(V))
                    dma(AP(x_cur.tensor, (b * D + dtile * 128) * S + t0, [[S, 128], [1, 512]]),
                        sl(sc3, (sub % 2) * 512, 128, 512, 2048))
                    P.wait("tensor", V, P.c(V))

        XSTB_t = xst.ap().bitcast(BF16).tensor

        def xbv(off, dims, p=128):
            return AP(XSTB_t, off, [[2 * 4096, p]] + dims)

        def s3(off, dims, p=128):
            return AP(sc3.ap().tensor, off, [[2048, p]] + dims)

        def emit_ffn(b):
            emit_ln(b, AT_LNX, gate=True)
            P.wait("gpsimd", T)
            dma_cast(sl(wsl, 0, 128, 4096, 4096), B1c)
            dma_cast(sl(wsl2, 0, 128, 4096, 4096), B3c)
            dma_cast(bv(FF_A2C, [[1, 4096]]), A2cB)
            dma_cast(bv(FF_B2C, [[1, 1024]]), B2c)
            dma_cast(bv(FF_A1CT, [[1, 1024]]), A1cB)
            dma_cast(bv(FF_A3CT, [[1, 1024]]), A3cB)
            c_slab = P.c(LG)

            ln_done = P.c(V)
            P.wait("tensor", V, ln_done)
            P.wait("tensor", SS, P.c(SS))
            for ct in range(16):
                P.op("tensor", lambda e, ct=ct: e.transpose(
                    out=PS[1][:, ct * 8:(ct + 1) * 8],
                    in_=sl(sc4, ct * 128, 8, 128, 2048),
                    identity=cI[0:8, 0:8]), inc=(T if ct == 15 else None))
            P.wait("vector", T, P.c(T))

            def vstep(fn):
                P.op("vector", fn, inc=V)
                P.wait("vector", V, P.c(V))

            vstep(lambda e: e.tensor_copy(s3(0, [[1, 128]]), PS[1][:, 0:128]))
            vstep(lambda e: e.tensor_tensor(
                s3(1024, [[4, 16], [1, 4]]),
                s3(0, [[8, 16], [1, 4]]), s3(4, [[8, 16], [1, 4]]), ALU.max))
            vstep(lambda e: e.tensor_tensor(
                s3(1088, [[2, 16], [1, 2]]),
                s3(1024, [[4, 16], [1, 2]]), s3(1026, [[4, 16], [1, 2]]), ALU.max))
            vstep(lambda e: e.tensor_tensor(
                s3(512, [[1, 16]]),
                s3(1088, [[2, 16], [1, 1]]), s3(1089, [[2, 16], [1, 1]]), ALU.max))
            vstep(lambda e: e.tensor_tensor(
                s3(128, [[1, 128]]), s3(0, [[1, 128]]),
                s3(512, [[1, 16], [0, 8]]), ALU.is_equal))
            vstep(lambda e: e.scalar_tensor_tensor(
                s3(384, [[1, 128]]), s3(128, [[1, 128]]), -1e9,
                s3(0, [[1, 128]]), ALU.mult, ALU.add))
            vstep(lambda e: e.tensor_tensor(
                s3(1024, [[4, 16], [1, 4]]),
                s3(384, [[8, 16], [1, 4]]), s3(388, [[8, 16], [1, 4]]), ALU.max))
            vstep(lambda e: e.tensor_tensor(
                s3(1088, [[2, 16], [1, 2]]),
                s3(1024, [[4, 16], [1, 2]]), s3(1026, [[4, 16], [1, 2]]), ALU.max))
            vstep(lambda e: e.tensor_tensor(
                s3(528, [[1, 16]]),
                s3(1088, [[2, 16], [1, 1]]), s3(1089, [[2, 16], [1, 1]]), ALU.max))
            vstep(lambda e: e.tensor_tensor(
                s3(256, [[1, 128]]), s3(384, [[1, 128]]),
                s3(528, [[1, 16], [0, 8]]), ALU.is_equal))
            vstep(lambda e: e.tensor_sub(
                s3(832, [[1, 16]]), s3(512, [[1, 16]]), s3(528, [[1, 16]])))
            P.wait("scalar", V, P.c(V))
            P.op("scalar", lambda e: e.activation(
                out=s3(544, [[1, 16]]), in_=s3(832, [[1, 16]]), func=AF.Sigmoid), inc=SS)
            P.wait("scalar", SS, P.c(SS))
            P.op("scalar", lambda e: e.activation(
                out=s3(560, [[1, 16]]), in_=s3(832, [[1, 16]]), func=AF.Sigmoid,
                scale=-1.0), inc=SS)
            P.wait("vector", SS, P.c(SS))
            vstep(lambda e: e.tensor_mul(
                s3(576, [[1, 128]]), s3(128, [[1, 128]]), s3(544, [[1, 16], [0, 8]])))
            vstep(lambda e: e.tensor_mul(
                s3(704, [[1, 128]]), s3(256, [[1, 128]]), s3(560, [[1, 16], [0, 8]])))
            exps = [(128, 0, FF_M1X, cI2), (256, 0, FF_M1X + 2048, cI2),
                    (576, 0, FF_M1X + 4096, cI2), (704, 0, FF_M1X + 6144, cI2),
                    (544, 1, FF_W1B, cI), (560, 1, FF_W1B + 2048, cI)]
            P.wait("vector", V, P.c(V))
            for src, w16, dst, rhs in exps:
                for g in range(4):
                    P.wait("vector", T, P.c(T))
                    for j in range(4):
                        ct = g * 4 + j
                        if w16:
                            iap = s3(src + ct, [[0, 128]])
                        else:
                            iap = s3(src + ct * 8, [[1, 8], [0, 16]])
                        P.op("vector", lambda e, iap=iap, j=j: e.tensor_copy(
                            AP(sc2.ap().tensor, j * 128, [[2048, 128], [1, 128]]), iap), inc=V)
                        P.wait("tensor", V, P.c(V))
                        if P.c(SS) > 0:
                            P.wait("tensor", SS, P.c(SS))
                        P.op("tensor", lambda e, j=j, rhs=rhs: e.matmul(
                            PS[2][:, j * 128:(j + 1) * 128],
                            AP(sc2.ap().tensor, j * 128, [[2048, 128], [1, 128]]),
                            rhs[:, :], start=True, stop=True), inc=(T if j == 3 else None))
                    P.wait("scalar", T, P.c(T))
                    P.op("scalar", lambda e, dst=dst, g=g: e.activation(
                        out=bv(dst + g * 512, [[1, 512]]), in_=PS[2][:, :],
                        func=AF.Copy), inc=SS)
            P.wait("tensor", SS, P.c(SS))
            P.wait("vector", SS, P.c(SS))
            if DEBUG == 2 and b == 0:
                P.barrier()
                dma_st(dbg["ffn1"], sl(sc3, 0, 128, 1152, 2048))
                dma_st(dbg["ffn3"], AP(ARNB_t, 32768, [[2 * ARN_COLS, 128], [1, 8192]]))
                P.wait("sync", STQ, P.c(STQ))

            for tq in range(4):
                for c in range(8):
                    P.op("tensor", lambda e, c=c, tq=tq: e.matmul(
                        PS[0][:, :], bv(FF_A1CT + c * 128, [[1, 128]]),
                        lnx_sl(c, tq * 512, 512), start=(c == 0), stop=(c == 7)))
                    P.op("tensor", lambda e, c=c, tq=tq: e.matmul(
                        PS[1][:, :], bv(FF_A3CT + c * 128, [[1, 128]]),
                        lnx_sl(c, tq * 512, 512), start=(c == 0), stop=(c == 7)),
                        inc=(T if c == 7 else None))
                P.wait("scalar", T, P.c(T))
                P.op("scalar", lambda e, tq=tq: e.activation(
                    out=bv(FF_XA1 + tq * 512, [[1, 512]]), in_=PS[0][:, :], func=AF.Copy))
                P.op("scalar", lambda e, tq=tq: e.activation(
                    out=bv(FF_XA3 + tq * 512, [[1, 512]]), in_=PS[1][:, :], func=AF.Copy), inc=SS)
                P.wait("tensor", SS, P.c(SS))
            P.wait("vector", SS, P.c(SS))
            for i, (xa, msk) in enumerate([(FF_XA1, FF_M1X), (FF_XA3, FF_M1X),
                                           (FF_XA1, FF_M1X + 2048), (FF_XA3, FF_M1X + 2048)]):
                P.op("vector", lambda e, xa=xa, msk=msk, i=i: e.tensor_mul(
                    bv(FF_XM + i * 2048, [[1, 2048]]),
                    bv(xa, [[1, 2048]]), bv(msk, [[1, 2048]])), inc=V)

            if DEBUG == 2 and b == 0:
                P.barrier()
                dma_st(dbg["ffn2"], AP(ARNB_t, 16384, [[2 * ARN_COLS, 128], [1, 16384]]))
                P.wait("sync", STQ, P.c(STQ))

            TMP_A1, TMP_A2, TMP_S1, TMP_T3 = FF_A1, FF_A1 + 512, FF_A1 + 1024, FF_A1 + 1536
            xam_off = [FF_XM, FF_XM + 2048, FF_XM + 4096, FF_XM + 6144]
            for tq in range(4):
                t0 = tq * 512
                P.wait("gpsimd", T, P.c(T))
                P.wait("gpsimd", V, P.c(V))
                tmark = {}
                vmark = {}
                smark = {}
                for fc in range(32):
                    buf = fc % 2
                    if fc >= 2:
                        P.wait("gpsimd", T, tmark[fc - 2])
                    sgn = "lgA" if buf == 0 else "lgB"
                    dma_cast(xbv(buf * 1024, [[1, 1024]]),
                             AP(WgB.tensor, fc * 128 * 1024, [[1024, 128], [1, 1024]]), semn=sgn)
                    cw = dma_cast(xbv(2048 + buf * 1024, [[1, 1024]]),
                                  AP(WuB.tensor, fc * 128 * 1024, [[1024, 128], [1, 1024]]), semn=sgn)
                    P.wait("tensor", sgn, cw)
                    if fc >= 1:
                        P.wait("tensor", V, vmark[fc - 1])
                        P.wait("tensor", SS, smark[fc - 1])
                    else:
                        P.wait("tensor", V, P.c(V))
                        P.wait("tensor", SS, P.c(SS))
                    for c in range(8):
                        P.op("tensor", lambda e, c=c, buf=buf, t0=t0: e.matmul(
                            PS[0][:, :], xbv(buf * 1024 + c * 128, [[1, 128]]),
                            lnx_sl(c, t0, 512), start=(c == 0), stop=(c == 7)))
                        P.op("tensor", lambda e, c=c, buf=buf, t0=t0: e.matmul(
                            PS[1][:, :], xbv(2048 + buf * 1024 + c * 128, [[1, 128]]),
                            lnx_sl(c, t0, 512), start=(c == 0), stop=(c == 7)))
                    P.op("tensor", lambda e, fc=fc, t0=t0: e.matmul(
                        PS[2][:, :], sl(wsl, fc * 128, 128, 128, 4096),
                        bv(xam_off[0] + t0, [[1, 512]]), start=True, stop=True))
                    P.op("tensor", lambda e, fc=fc, t0=t0: e.matmul(
                        PS[3][:, :], sl(wsl, fc * 128, 128, 128, 4096),
                        bv(xam_off[2] + t0, [[1, 512]]), start=True, stop=True))
                    P.op("tensor", lambda e, fc=fc, t0=t0: e.matmul(
                        PS[4][:, :], sl(wsl2, fc * 128, 128, 128, 4096),
                        bv(xam_off[1] + t0, [[1, 512]]), start=True, stop=True))
                    tmark[fc] = P.op("tensor", lambda e, fc=fc, t0=t0: e.matmul(
                        PS[5][:, :], sl(wsl2, fc * 128, 128, 128, 4096),
                        bv(xam_off[3] + t0, [[1, 512]]), start=True, stop=True), inc=T)
                    P.wait("scalar", T, tmark[fc])
                    if fc >= 1:
                        P.wait("scalar", V, vmark[fc - 1])
                    P.op("scalar", lambda e: e.activation(
                        out=sl(sc1, 0, 128, 512, 2048), in_=PS[0][:, :], func=AF.Copy))
                    sm = P.op("scalar", lambda e: e.activation(
                        out=sl(sc1, 512, 128, 512, 2048), in_=PS[1][:, :], func=AF.Copy), inc=SS)
                    for slot in range(2):
                        d1b, d3b = (2, 4) if slot == 0 else (3, 5)
                        tmpa = TMP_A1 if slot == 0 else TMP_A2
                        P.wait("vector", SS, sm)
                        P.wait("vector", T, tmark[fc])
                        P.op("vector", lambda e, d1b=d1b: e.tensor_add(
                            sl(sc2, 512, 128, 512, 2048), PS[d1b][:, :],
                            sl(sc1, 0, 128, 512, 2048)), inc=V)
                        P.wait("scalar", V, P.c(V))
                        sl_s1 = P.op("scalar", lambda e: e.activation(
                            out=bv(TMP_S1, [[1, 512]]), in_=sl(sc2, 512, 128, 512, 2048),
                            func=AF.Silu), inc=SS)
                        P.op("vector", lambda e, d3b=d3b: e.tensor_add(
                            bv(TMP_T3, [[1, 512]]), PS[d3b][:, :],
                            sl(sc1, 512, 128, 512, 2048)))
                        P.wait("vector", SS, sl_s1)
                        P.op("vector", lambda e, tmpa=tmpa: e.tensor_mul(
                            bv(tmpa, [[1, 512]]), bv(TMP_S1, [[1, 512]]),
                            bv(TMP_T3, [[1, 512]])), inc=V)
                    smark[fc] = P.c(SS)
                    P.op("vector", lambda e, fc=fc, t0=t0: e.tensor_mul(
                        bv(FF_WACT + fc * 512, [[1, 512]]),
                        bv(TMP_A1, [[1, 512]]), bv(FF_W1B + t0, [[1, 512]])))
                    P.op("vector", lambda e, t0=t0: e.tensor_mul(
                        bv(TMP_T3, [[1, 512]]),
                        bv(TMP_A2, [[1, 512]]), bv(FF_W1B + 2048 + t0, [[1, 512]])))
                    P.op("vector", lambda e, fc=fc: e.tensor_add(
                        bv(FF_WACT + fc * 512, [[1, 512]]),
                        bv(FF_WACT + fc * 512, [[1, 512]]), bv(TMP_T3, [[1, 512]])), inc=V)
                    vmark[fc] = P.c(V)
                    P.wait("tensor", V, vmark[fc])
                    if fc == 0:
                        P.wait("tensor", V, P.c(V))
                    P.op("tensor", lambda e, fc=fc: e.matmul(
                        PS[6][:, :], bv(FF_A2C + fc * 128, [[1, 128]]),
                        bv(TMP_A1, [[1, 512]]), start=(fc == 0), stop=(fc == 31)))
                    zzm = P.op("tensor", lambda e, fc=fc: e.matmul(
                        PS[7][:, :], bv(FF_A2C + fc * 128, [[1, 128]]),
                        bv(TMP_A2, [[1, 512]]), start=(fc == 0), stop=(fc == 31)),
                        inc=(T if fc == 31 else None))
                P.wait("vector", T, P.c(T))
                P.op("vector", lambda e, t0=t0: e.tensor_mul(
                    bv(FF_ZW, [[1, 512]]), PS[6][:, :], bv(FF_M1X + 4096 + t0, [[1, 512]])))
                P.op("vector", lambda e, t0=t0: e.tensor_mul(
                    bv(FF_ZW + 512, [[1, 512]]), PS[7][:, :],
                    bv(FF_M1X + 6144 + t0, [[1, 512]])), inc=V)
                zw_done = P.c(V)
                if DEBUG == 2 and b == 0 and tq == 2:
                    P.barrier()
                    dma_st(dbg["ffn4"], AP(ARNB_t, 40960, [[2 * ARN_COLS, 128], [1, 19456]]))
                    P.wait("sync", STQ, P.c(STQ))
                dmark = {}
                xmark = {}
                stmark = {}
                for dc in range(8):
                    pb = dc % 2
                    if dc >= 1:
                        P.wait("gpsimd", T, dmark[dc - 1])
                    cw = dma_cast(xbv(4096, [[1, 4096]]),
                                  AP(WdB.tensor, dc * 128 * 4096, [[4096, 128], [1, 4096]]),
                                  semn="lgW")
                    if dc == 0:
                        P.wait("sync", SS, P.c(SS))
                        P.wait("sync", V, P.c(V))
                    if dc >= 2:
                        P.wait("sync", V, xmark[dc - 2])
                    c_x = dma(sl(sc2, (dc % 2) * 512, 128, 512, 2048),
                              AP(x_cur.tensor, (b * D + dc * 128) * S + t0,
                                 [[S, 128], [1, 512]]))
                    P.wait("tensor", "lgW", cw)
                    P.wait("tensor", V, zw_done)
                    if dc == 0:
                        P.wait("tensor", SS, P.c(SS))
                    if dc >= 2:
                        P.wait("tensor", V, xmark[dc - 2])
                    for fc in range(32):
                        P.op("tensor", lambda e, fc=fc, pb=pb: e.matmul(
                            PS[pb][:, :], xbv(4096 + fc * 128, [[1, 128]]),
                            bv(FF_WACT + fc * 512, [[1, 512]]),
                            start=(fc == 0), stop=False))
                    P.op("tensor", lambda e, dc=dc, pb=pb: e.matmul(
                        PS[pb][:, :], bv(FF_B2C + dc * 128, [[1, 128]]),
                        bv(FF_ZW, [[1, 512]]), start=False, stop=False))
                    dmark[dc] = P.op("tensor", lambda e, dc=dc, pb=pb: e.matmul(
                        PS[pb][:, :], bv(FF_B2C + dc * 128, [[1, 128]]),
                        bv(FF_ZW + 512, [[1, 512]]), start=False, stop=True), inc=T)
                    P.wait("vector", T, dmark[dc])
                    P.wait("vector", LD, c_x)
                    if dc >= 2:
                        P.wait("vector", STQ, stmark[dc - 2])
                    P.op("vector", lambda e, dc=dc, pb=pb: e.tensor_add(
                        sl(sc2, 1024 + (dc % 2) * 512, 128, 512, 2048),
                        PS[pb][:, :], sl(sc2, (dc % 2) * 512, 128, 512, 2048)), inc=V)
                    xmark[dc] = P.c(V)
                    P.wait("sync", V, xmark[dc])
                    dma_st(AP(x_cur.tensor, (b * D + dc * 128) * S + t0, [[S, 128], [1, 512]]),
                           sl(sc2, 1024 + (dc % 2) * 512, 128, 512, 2048))
                    stmark[dc] = P.c(STQ)

        MM_U = 0
        MM_XI = 32768
        MM_B = 32768
        MM_C = 49152
        MM_XD = 65600
        SC4B_t = sc4.ap().bitcast(BF16).tensor
        SC3B_t = sc3.ap().bitcast(BF16).tensor
        BTRB_t = btr.ap().bitcast(BF16).tensor

        def arn_b(off, w, p=128):
            return AP(ARNB_t, off, [[2 * ARN_COLS, p], [1, w]])

        def emit_mamba(b):
            for dr in range(2):
                emit_ln(b, AT_LNX)
                ln_done = P.c(V)
                c_al = dma(sl(sc3, 0, 128, 256, 2048),
                           AP(AlogB.tensor, dr * 128 * 256, [[256, 128], [1, 256]]))
                dma(sl(sc1, 256, 128, 64, 2048),
                    AP(convWB.tensor, dr * 128 * 64, [[64, 128], [1, 64]]))
                dma(sl(sc1, 320, 128, 16, 2048),
                    AP(convBv.tensor, dr * 128 * 16, [[16, 128], [1, 16]]))
                dma(sl(sc1, 336, 128, 16, 2048),
                    AP(dtBv.tensor, dr * 128 * 16, [[16, 128], [1, 16]]))
                c_cn = dma(sl(sc1, 352, 128, 16, 2048),
                           AP(Dpv.tensor, dr * 128 * 16, [[16, 128], [1, 16]]))
                P.wait("scalar", LD, c_al)
                P.op("scalar", lambda e: e.activation(
                    out=sl(sc3, 256, 128, 256, 2048), in_=sl(sc3, 0, 128, 256, 2048),
                    func=AF.Exp), inc=SS)
                P.wait("vector", SS, P.c(SS))
                P.op("vector", lambda e: e.tensor_scalar(
                    sl(sc1, 0, 128, 256, 2048), sl(sc3, 256, 128, 256, 2048),
                    -1.0, None, ALU.mult), inc=V)
                P.op("vector", lambda e: e.memset(
                    AP(ARNB_t, MM_XI, [[2 * ARN_COLS, 128], [2052, 16], [1, 3]]), 0.0), inc=V)

                m1_t = {}
                m1_s = {}
                zst = {}
                for ec in range(32):
                    buf = ec % 2
                    if ec >= 2:
                        P.wait("gpsimd", T, m1_t[ec - 2])
                    sgn = "lgA" if buf == 0 else "lgB"
                    cw = dma_cast(sl(wsl, buf * 1024, 128, 1024, 4096),
                                  AP(inWB.tensor, (dr * 32 + ec) * 128 * 1024,
                                     [[1024, 128], [1, 1024]]), semn=sgn)
                    P.wait("tensor", sgn, cw)
                    P.wait("tensor", V, ln_done)
                    if ec >= 1:
                        P.wait("tensor", SS, m1_s[ec - 1])
                    if ec == 0:
                        P.wait("tensor", SS, P.c(SS))
                    for tq in range(4):
                        t0 = tq * 512
                        for c in range(8):
                            if dr == 0:
                                rap = AP(ARNB_t, c * 2048 + t0, [[2 * ARN_COLS, 128], [1, 512]])
                            else:
                                rap = AP(ARNB_t, c * 2048 + (2047 - t0), [[2 * ARN_COLS, 128], [-1, 512]])
                            P.op("tensor", lambda e, tq=tq, c=c, buf=buf, rap=rap: e.matmul(
                                PS[tq][:, :], sl(wsl, buf * 1024 + c * 128, 128, 128, 4096),
                                rap, start=(c == 0), stop=(c == 7)),
                                inc=(T if (tq == 3 and c == 7) else None))
                    m1_t[ec] = P.c(T)
                    P.wait("scalar", T, m1_t[ec])
                    if ec >= 18:
                        P.wait("scalar", STQ, zst[ec - 2])
                    for tq in range(4):
                        if ec < 16:
                            P.op("scalar", lambda e, ec=ec, tq=tq: e.activation(
                                out=arn_b(MM_XI + ec * 2052 + 3 + tq * 512, 512),
                                in_=PS[tq][:, :], func=AF.Copy),
                                inc=(SS if tq == 3 else None))
                        else:
                            P.op("scalar", lambda e, ec=ec, tq=tq: e.activation(
                                out=AP(SC4B_t, (ec % 2) * 2048 + tq * 512, [[4096, 128], [1, 512]]),
                                in_=PS[tq][:, :], func=AF.Silu),
                                inc=(SS if tq == 3 else None))
                    m1_s[ec] = P.c(SS)
                    if ec >= 16:
                        P.wait("sync", SS, m1_s[ec])
                        zst[ec] = dma_st(
                            AP(mm_z.tensor, (ec - 16) * 128 * 2048, [[2048, 128], [1, 2048]]),
                            AP(SC4B_t, (ec % 2) * 2048, [[4096, 128], [1, 2048]]))

                P.barrier()
                cvm = {}
                for c in range(16):
                    base = MM_XI + c * 2052
                    if c >= 1:
                        P.wait("vector", SS, cvm[c - 1])
                    P.op("vector", lambda e, base=base, c=c: e.tensor_scalar(
                        sl(sc3, 0, 128, 2048, 2048), arn_b(base + 0, 2048),
                        sl(sc1, 256 + c * 4 + 0, 128, 1, 2048), None, ALU.mult))
                    for j in range(1, 4):
                        P.op("vector", lambda e, base=base, c=c, j=j: e.scalar_tensor_tensor(
                            sl(sc3, 0, 128, 2048, 2048), arn_b(base + j, 2048),
                            sl(sc1, 256 + c * 4 + j, 128, 1, 2048),
                            sl(sc3, 0, 128, 2048, 2048), ALU.mult, ALU.add),
                            inc=(V if j == 3 else None))
                    P.wait("scalar", V, P.c(V))
                    cvm[c] = P.op("scalar", lambda e, c=c: e.activation(
                        out=arn_b(MM_U + c * 2048, 2048), in_=sl(sc3, 0, 128, 2048, 2048),
                        func=AF.Silu, bias=sl(sc1, 320 + c, 128, 1, 2048)), inc=SS)

                P.wait("gpsimd", T, P.c(T))
                cw = dma_cast(sl(wsl2, 0, 128, 2048, 4096),
                              AP(xpWB.tensor, dr * 128 * 2048, [[2048, 128], [1, 2048]]))
                dma_cast(AP(wsl.ap().tensor, 64 * 4096, [[4096, 64], [1, 2048]]),
                         AP(dtWT.tensor, dr * 64 * 2048, [[2048, 64], [1, 2048]]))
                cw_dtw = P.c(LG)
                P.wait("tensor", LG, cw)
                P.wait("tensor", SS, P.c(SS))
                for q in range(4):
                    for c in range(16):
                        P.op("tensor", lambda e, q=q, c=c: e.matmul(
                            PS[q][:, :], sl(wsl2, c * 128, 128, 128, 4096),
                            arn_b(MM_U + c * 2048 + q * 512, 512),
                            start=(c == 0), stop=(c == 15)), inc=(T if c == 15 else None))
                P.wait("scalar", T, P.c(T))
                for q in range(4):
                    P.op("scalar", lambda e, q=q: e.activation(
                        out=arn_b(MM_XD + q * 512, 512), in_=PS[q][:, :], func=AF.Copy),
                        inc=(SS if q == 3 else None))
                P.wait("sync", SS, P.c(SS))
                dma_st(AP(mm_bc.tensor, 0, [[2048, 16], [1, 2048]]),
                       AP(ARNB_t, MM_XD, [[2 * ARN_COLS, 16], [1, 2048]]))
                dma_st(AP(mm_bc.tensor, 16 * 2048, [[2048, 16], [1, 2048]]),
                       AP(ARNB_t, MM_XD + 32 * 2 * ARN_COLS, [[2 * ARN_COLS, 16], [1, 2048]]))
                P.barrier()

                for th in range(2):
                    tb = th * 1024
                    cbm = None
                    for n in range(16):
                        dma(arn_b(MM_B + n * 1024, 1024),
                            AP(mm_bc.tensor, n * 2048 + tb, [[0, 128], [1, 1024]]))
                        cbm = dma(arn_b(MM_C + n * 1024, 1024),
                                  AP(mm_bc.tensor, 16 * 2048 + n * 2048 + tb, [[0, 128], [1, 1024]]))
                    P.wait("vector", LD, cbm)
                    zpre = {}
                    gate_v = {}
                    for c in range(16):
                        if c >= 2:
                            P.wait("sync", V, gate_v[c - 2])
                        zpre[c] = dma(AP(SC4B_t, (c % 2) * 2048, [[4096, 128], [1, 1024]]),
                                      AP(mm_z.tensor, c * 128 * 2048 + tb, [[2048, 128], [1, 1024]]))
                        if c >= 1:
                            P.wait("tensor", SS, sp_m)
                        P.wait("tensor", LG, cw_dtw)
                        for q in range(2):
                            P.op("tensor", lambda e, q=q, c=c: e.matmul(
                                PS[2 + q][:, :],
                                AP(wsl.ap().tensor, 64 * 4096 + c * 128, [[4096, 64], [1, 128]]),
                                AP(ARNB_t, 64 * 2 * ARN_COLS + MM_XD + tb + q * 512,
                                   [[2 * ARN_COLS, 64], [1, 512]]),
                                start=True, stop=True), inc=(T if q == 1 else None))
                        dmm = P.c(T)
                        P.wait("scalar", T, dmm)
                        if c >= 1:
                            P.wait("scalar", V, du_m)
                        for q in range(2):
                            P.op("scalar", lambda e, q=q, c=c: e.activation(
                                out=sl(sc3, q * 512, 128, 512, 2048), in_=PS[2 + q][:, :],
                                func=AF.Relu, bias=sl(sc1, 336 + c, 128, 1, 2048)),
                                inc=(SS if q == 1 else None))
                        sp_m = P.c(SS)
                        P.wait("vector", SS, sp_m)
                        for q in range(2):
                            P.op("vector", lambda e, q=q, c=c: e.tensor_scalar(
                                sl(sc3, 1024 + q * 512, 128, 512, 2048), PS[2 + q][:, :],
                                sl(sc1, 336 + c, 128, 1, 2048), None, ALU.add),
                                inc=(V if q == 1 else None))
                        P.op("vector", lambda e: e.scalar_tensor_tensor(
                            sl(sc2, 1024, 128, 1024, 2048), sl(sc3, 0, 128, 1024, 2048),
                            -2.0, sl(sc3, 1024, 128, 1024, 2048), ALU.mult, ALU.add), inc=V)
                        P.wait("scalar", V, P.c(V))
                        P.op("scalar", lambda e: e.activation(
                            out=sl(sc2, 1024, 128, 1024, 2048), in_=sl(sc2, 1024, 128, 1024, 2048),
                            func=AF.Exp))
                        P.op("scalar", lambda e: e.activation(
                            out=sl(sc2, 1024, 128, 1024, 2048), in_=sl(sc2, 1024, 128, 1024, 2048),
                            func=AF.Ln, bias=1.0), inc=SS)
                        P.wait("vector", SS, P.c(SS))
                        P.op("vector", lambda e: e.tensor_add(
                            sl(sc2, 0, 128, 1024, 2048), sl(sc3, 0, 128, 1024, 2048),
                            sl(sc2, 1024, 128, 1024, 2048)), inc=V)
                        P.wait("vector", V, P.c(V))
                        P.op("vector", lambda e, c=c, tb=tb: e.tensor_mul(
                            AP(SC2B_t, 2048, [[4096, 128], [1, 1024]]),
                            sl(sc2, 0, 128, 1024, 2048),
                            arn_b(MM_U + c * 2048 + tb, 1024)), inc=V)
                        du_m = P.c(V)
                        imm = 0
                        pv_last = 0
                        for n in range(16):
                            P.wait("scalar", V, du_m if n == 0 else P.c(V))
                            am = P.op("scalar", lambda e, c=c, n=n: e.activation(
                                out=sl(sc3, 0, 128, 1024, 2048), in_=sl(sc2, 0, 128, 1024, 2048),
                                func=AF.Exp, scale=sl(sc1, c * 16 + n, 128, 1, 2048)), inc=SS)
                            if n > 0:
                                P.wait("vector", T, imm)
                            P.op("vector", lambda e, n=n: e.tensor_tensor(
                                AP(SC2B_t, 3072, [[4096, 128], [1, 1024]]),
                                AP(SC2B_t, 2048, [[4096, 128], [1, 1024]]),
                                arn_b(MM_B + n * 1024, 1024), ALU.mult), inc=V)
                            P.wait("vector", SS, am)
                            if pv_last:
                                P.wait("vector", "PV", pv_last)
                            init = 0.0 if th == 0 else sl(sc1, 368 + c * 16 + n, 128, 1, 2048)
                            P.op("vector", lambda e, init=init: e.tensor_tensor_scan(
                                out=AP(SC3B_t, 2048, [[4096, 128], [1, 1024]]),
                                data0=sl(sc3, 0, 128, 1024, 2048),
                                data1=AP(SC2B_t, 3072, [[4096, 128], [1, 1024]]),
                                initial=init, op0=ALU.mult, op1=ALU.add), inc=V)
                            scm = P.c(V)
                            if th == 0:
                                P.wait("vector", V, scm)
                                P.op("vector", lambda e, c=c, n=n: e.tensor_copy(
                                    sl(sc1, 368 + c * 16 + n, 128, 1, 2048),
                                    AP(SC3B_t, 2048 + 1023, [[4096, 128], [1, 1]])), inc=V)
                            hco = (n % 2) * 1024
                            P.op("vector", lambda e, hco=hco, n=n: e.tensor_tensor(
                                AP(BTRB_t, hco, [[2048, 128], [1, 1024]]),
                                AP(SC3B_t, 2048, [[4096, 128], [1, 1024]]),
                                arn_b(MM_C + n * 1024, 1024), ALU.mult), inc=V)
                            hcm = P.c(V)
                            P.wait("tensor", V, hcm)
                            for q in range(2):
                                P.op("tensor", lambda e, q=q, n=n, hco=hco: e.matmul(
                                    PS[q][:, :], sl(wsm, 0, 128, 128, 1024),
                                    AP(BTRB_t, hco + q * 512, [[2048, 128], [1, 512]]),
                                    start=(n == 0), stop=(n == 15)), inc=(T if q == 1 else None))
                            imm = P.c(T)
                        P.wait("vector", T, imm)
                        P.wait("vector", LD, zpre[c])
                        for q in range(2):
                            P.op("vector", lambda e, c=c, q=q, tb=tb: e.scalar_tensor_tensor(
                                sl(sc2, 1024 + q * 512, 128, 512, 2048),
                                arn_b(MM_U + c * 2048 + tb + q * 512, 512),
                                sl(sc1, 352 + c, 128, 1, 2048),
                                PS[q][:, :], ALU.mult, ALU.add), inc=V)
                        P.wait("vector", V, P.c(V))
                        for q in range(2):
                            P.op("vector", lambda e, c=c, q=q, tb=tb: e.tensor_mul(
                                arn_b(MM_U + c * 2048 + tb + q * 512, 512),
                                sl(sc2, 1024 + q * 512, 128, 512, 2048),
                                AP(SC4B_t, (c % 2) * 2048 + q * 512, [[4096, 128], [1, 512]])),
                                inc=(V if q == 1 else None))
                        du_m = P.c(V)
                        gate_v[c] = du_m
                    P.barrier()

                P.wait("gpsimd", T, P.c(T))
                dmark = {}
                vmark = {}
                stm = {}
                for dc in range(8):
                    buf = dc % 2
                    if dc >= 2:
                        P.wait("gpsimd", T, dmark[dc - 2])
                    sgn = "lgA" if buf == 0 else "lgB"
                    cw = dma_cast(sl(wsl, buf * 2048, 128, 2048, 4096),
                                  AP(moWB.tensor, (dr * 8 + dc) * 128 * 2048,
                                     [[2048, 128], [1, 2048]]), semn=sgn)
                    P.wait("tensor", sgn, cw)
                    for tq in range(4):
                        t0 = tq * 512
                        pb = 4 + (tq % 2)
                        if dr == 1:
                            c_x = dma(sl(sc3, 512 + (tq % 2) * 512, 128, 512, 2048),
                                      AP(x_cur.tensor, (b * D + dc * 128) * S + t0,
                                         [[S, 128], [1, 512]]))
                            c_yp = dma(sl(sc2, (tq % 2) * 512, 128, 512, 2048),
                                       AP(mm_yp.tensor, dc * 128 * S + t0, [[S, 128], [1, 512]]))
                        kk = (dc, tq)
                        if (dc, tq - 2) in vmark:
                            P.wait("tensor", SS, P.c(SS))
                            P.wait("tensor", V, vmark[(dc, tq - 2)])
                        elif dc >= 1 and (dc - 1, tq + 2) in vmark:
                            P.wait("tensor", SS, P.c(SS))
                            P.wait("tensor", V, vmark[(dc - 1, tq + 2)])
                        for c in range(16):
                            if dr == 0:
                                rap = arn_b(MM_U + c * 2048 + t0, 512)
                            else:
                                rap = AP(ARNB_t, MM_U + c * 2048 + (2047 - t0),
                                         [[2 * ARN_COLS, 128], [-1, 512]])
                            P.op("tensor", lambda e, c=c, buf=buf, rap=rap, pb=pb: e.matmul(
                                PS[pb][:, :], sl(wsl, buf * 2048 + c * 128, 128, 128, 4096),
                                rap, start=(c == 0), stop=(c == 15)),
                                inc=(T if c == 15 else None))
                        mm_done = P.c(T)
                        if dr == 0:
                            P.wait("scalar", T, mm_done)
                            if (dc, tq - 2) in stm:
                                P.wait("scalar", STQ, stm[(dc, tq - 2)])
                            elif dc >= 1 and (dc - 1, tq + 2) in stm:
                                P.wait("scalar", STQ, stm[(dc - 1, tq + 2)])
                            P.op("scalar", lambda e, tq=tq, pb=pb: e.activation(
                                out=sl(sc3, (tq % 2) * 512, 128, 512, 2048),
                                in_=PS[pb][:, :], func=AF.Copy), inc=SS)
                            vmark[kk] = 0
                            P.wait("sync", SS, P.c(SS))
                            stm[kk] = dma_st(
                                AP(mm_yp.tensor, dc * 128 * S + t0, [[S, 128], [1, 512]]),
                                sl(sc3, (tq % 2) * 512, 128, 512, 2048))
                        else:
                            P.wait("vector", T, mm_done)
                            P.wait("vector", LD, c_yp)
                            P.op("vector", lambda e, tq=tq, pb=pb: e.tensor_add(
                                sl(sc2, 1024 + (tq % 2) * 512, 128, 512, 2048),
                                PS[pb][:, :], sl(sc2, (tq % 2) * 512, 128, 512, 2048)))
                            P.wait("vector", LD, c_x)
                            if (dc, tq - 2) in stm:
                                P.wait("vector", STQ, stm[(dc, tq - 2)])
                            elif dc >= 1 and (dc - 1, tq + 2) in stm:
                                P.wait("vector", STQ, stm[(dc - 1, tq + 2)])
                            P.op("vector", lambda e, tq=tq: e.scalar_tensor_tensor(
                                sl(sc3, 1024 + (tq % 2) * 512, 128, 512, 2048),
                                sl(sc2, 1024 + (tq % 2) * 512, 128, 512, 2048), 0.5,
                                sl(sc3, 512 + (tq % 2) * 512, 128, 512, 2048),
                                ALU.mult, ALU.add), inc=V)
                            vmark[kk] = P.c(V)
                            P.wait("sync", V, vmark[kk])
                            stm[kk] = dma_st(
                                AP(x_cur.tensor, (b * D + dc * 128) * S + t0, [[S, 128], [1, 512]]),
                                sl(sc3, 1024 + (tq % 2) * 512, 128, 512, 2048))
                    dmark[dc] = P.c(T)
                P.barrier()

        for b in range(BL):
            emit_attn(b)
        if N_SUBLAYERS >= 2:
            for b in range(BL):
                emit_ffn(b)
        if N_SUBLAYERS >= 3:
            for b in range(BL):
                emit_mamba(b)
        if N_SUBLAYERS >= 4:
            for b in range(BL):
                emit_ffn(b)

        P.barrier()
        c_out = dma_st(out_xT, x_cur)
        P.wait("sync", STQ, P.c(STQ))

        P.replay(block)
    return nc



def _prep_shared(i):
    f32 = np.float32
    o = {}
    o["rpe"] = np.ascontiguousarray(i["L0_attn_a_rpe_bias"], f32)
    qkvT = np.asarray(i["L0_attn_a_qkv"], f32).reshape(D, 3 * NH * DH)
    o["qkvT"] = np.ascontiguousarray(qkvT)
    o["qkvB"] = np.ascontiguousarray(qkvT.reshape(8, 128, 24, 128).transpose(2, 1, 0, 3).reshape(24, 128, 1024))
    o["outwT"] = np.ascontiguousarray(np.asarray(i["L0_attn_a_out_w"], f32).reshape(NH * DH, D))

    def mamba_dir(sfx):
        d = {}
        inWT = np.asarray(i[f"L0_attn_b_inW_{sfx}"], f32).T
        d["inWB"] = np.ascontiguousarray(inWT.reshape(8, 128, 32, 128).transpose(2, 1, 0, 3).reshape(32, 128, 1024))
        d["convWB"] = np.ascontiguousarray(np.asarray(i[f"L0_attn_b_convW_{sfx}"], f32)[:, 0, :].reshape(16, 128, 4).transpose(1, 0, 2).reshape(128, 64))
        d["convBv"] = np.ascontiguousarray(np.asarray(i[f"L0_attn_b_convB_{sfx}"], f32).reshape(16, 128).T)
        xp = np.asarray(i[f"L0_attn_b_xpW_{sfx}"], f32)
        pad = np.zeros((DI, 128), f32)
        pad[:, 0:16] = xp[64:80].T
        pad[:, 32:48] = xp[80:96].T
        pad[:, 64:128] = xp[0:64].T
        d["xpWB"] = np.ascontiguousarray(pad.reshape(16, 128, 128).transpose(1, 0, 2).reshape(128, 2048))
        d["dtWT"] = np.ascontiguousarray(np.asarray(i[f"L0_attn_b_dtW_{sfx}"], f32).T)
        d["dtBv"] = np.ascontiguousarray(np.asarray(i[f"L0_attn_b_dtB_{sfx}"], f32).reshape(16, 128).T)
        d["AlogB"] = np.ascontiguousarray(np.asarray(i[f"L0_attn_b_Alog_{sfx}"], f32).reshape(16, 128, 16).transpose(1, 0, 2).reshape(128, 256))
        d["Dpv"] = np.ascontiguousarray(np.asarray(i[f"L0_attn_b_Dp_{sfx}"], f32).reshape(16, 128).T)
        moWT = np.asarray(i[f"L0_attn_b_outW_{sfx}"], f32).T
        d["moWB"] = np.ascontiguousarray(moWT.reshape(16, 128, 8, 128).transpose(2, 1, 0, 3).reshape(8, 128, 2048))
        return d

    mf, mb = mamba_dir("f"), mamba_dir("b")
    for k in mf:
        o[k] = np.ascontiguousarray(np.stack([mf[k], mb[k]], axis=0))

    o["gateB"] = np.ascontiguousarray(np.asarray(i["L0_ffn_gate_w"], f32).T.reshape(8, 128, 8).transpose(1, 0, 2).reshape(128, 64))
    WgT = np.asarray(i["L0_ffn_w_gate"], f32).T
    WuT = np.asarray(i["L0_ffn_w_up"], f32).T
    WdT = np.asarray(i["L0_ffn_w_down"], f32).T
    o["WgB"] = np.ascontiguousarray(WgT.reshape(8, 128, 32, 128).transpose(2, 1, 0, 3).reshape(32, 128, 1024))
    o["WuB"] = np.ascontiguousarray(WuT.reshape(8, 128, 32, 128).transpose(2, 1, 0, 3).reshape(32, 128, 1024))
    o["WdB"] = np.ascontiguousarray(WdT.reshape(32, 128, 8, 128).transpose(2, 1, 0, 3).reshape(8, 128, 4096))
    o["A1cB"] = np.ascontiguousarray(np.asarray(i["L0_ffn_A1"], f32).reshape(128, D).T.reshape(8, 128, 128).transpose(1, 0, 2).reshape(128, 1024))
    o["A3cB"] = np.ascontiguousarray(np.asarray(i["L0_ffn_A3"], f32).reshape(128, D).T.reshape(8, 128, 128).transpose(1, 0, 2).reshape(128, 1024))
    o["B1c"] = np.ascontiguousarray(np.asarray(i["L0_ffn_B1"], f32).transpose(0, 2, 1).reshape(128, F))
    o["B3c"] = np.ascontiguousarray(np.asarray(i["L0_ffn_B3"], f32).transpose(0, 2, 1).reshape(128, F))
    o["A2cB"] = np.ascontiguousarray(np.asarray(i["L0_ffn_A2"], f32).transpose(2, 0, 1).reshape(32, 128, 128).transpose(1, 0, 2).reshape(128, 4096))
    o["B2c"] = np.ascontiguousarray(np.asarray(i["L0_ffn_B2"], f32).transpose(0, 2, 1).reshape(128, D))

    o["Jrev"] = np.eye(128, dtype=f32)[::-1].copy()
    o["Ident"] = np.eye(128, dtype=f32)
    o["Itwo"] = np.eye(128, dtype=f32) * 2.0
    o["onesc"] = np.ones((128, 1), f32)
    o["ones1r"] = np.ones((1, 128), f32)
    ex = np.zeros((E, 128), f32)
    for e in range(E):
        ex[e, e * 16:(e + 1) * 16] = 2.0
    o["exp16"] = ex
    cx = np.zeros((128, 8), f32)
    cx[:, 0] = EPS
    cx[:, 1] = 1.0 / D
    cx[:, 2] = -0.5
    cx[:, 3] = 0.125
    o["cextra"] = cx
    return o


def kernel(**inputs):
    shared = _prep_shared(inputs)
    x = np.asarray(inputs["L0_attn_a_x"], np.float32)
    nc = build_program()
    in_maps = []
    for c in range(8):
        m = dict(shared)
        xs = x[2 * c:2 * c + 2]
        m["xT"] = np.ascontiguousarray(xs.transpose(0, 2, 1))
        in_maps.append(m)
    res = run_bass_kernel_spmd(nc, in_maps, core_ids=list(range(8)))
    out = np.empty((16, S, D), np.float32)
    for c in range(8):
        o = np.asarray(res.results[c]["out_xT"])
        out[2 * c:2 * c + 2] = o.transpose(0, 2, 1)
    return out


# version 17
# speedup vs baseline: 1.2755x; 1.2755x over previous
import numpy as np
import ml_dtypes
import concourse.bass as bass
import concourse.mybir as mybir
from concourse.ap import AP
from concourse.bass_utils import run_bass_kernel_spmd
from contextlib import ExitStack

F32 = mybir.dt.float32
BF16 = mybir.dt.bfloat16
AF = mybir.ActivationFunctionType
ALU = mybir.AluOpType
AX = mybir.AxisListType

BL, S, D = 2, 2048, 1024
NH, DH = 16, 64
DI, NST, DTR = 2048, 16, 64
E, F, R = 8, 4096, 16
PAD = -30000.0
EPS = 1e-5

N_SUBLAYERS = 4
DEBUG = 0

ENGINES = ("sync", "tensor", "vector", "scalar", "gpsimd")

ARN_COLS = 34048
AT_LNX = 0
AT_Q = 16384
AT_K = 32768
AT_V = 49152
FF_XA1 = 16384
FF_XA3 = 18432
FF_XM = 20480
FF_M1X = 28672
FF_W1B = 36864
FF_WACT = 40960
FF_A1 = 57344
FF_ZW = 59392
FF_A2C = 60416
FF_B2C = 64512
FF_A1CT = 65536
FF_A3CT = 66560
FF_GATE = 67584


class Prog:
    def __init__(self, nc, ctx):
        self.nc, self.ctx = nc, ctx
        self.q = {e: [] for e in ENGINES}
        self.sems, self.cnt = {}, {}
        self.waited = {e: {} for e in ENGINES}

    def sem(self, name):
        if name not in self.sems:
            self.sems[name] = self.ctx.enter_context(self.nc.semaphore(name))
            self.cnt[name] = 0
        return self.sems[name]

    def wait(self, eng, sname, val=None):
        if val is None:
            val = self.cnt.get(sname, 0)
        if val <= 0:
            return
        h = self.sem(sname)
        if self.waited[eng].get(sname, -1) >= val:
            return
        self.waited[eng][sname] = val
        self.q[eng].append(lambda e, h=h, v=val: e.wait_ge(h, v))

    def op(self, eng, fn, inc=None, amt=1):
        if inc is None:
            self.q[eng].append(fn)
            return None
        h = self.sem(inc)
        self.cnt[inc] += amt
        self.q[eng].append(lambda e, fn=fn, h=h, a=amt: fn(e).then_inc(h, a))
        return self.cnt[inc]

    def c(self, sname):
        return self.cnt.get(sname, 0)

    def barrier(self, engs=ENGINES, sems=("T", "V", "S", "ld", "ldg", "st", "lgA", "lgB", "lgW", "PV", "lbc", "lz0", "lz1", "sz0", "sz1", "lx0", "lx1", "lt0", "lt1", "lt2", "lt3")):
        for eng in engs:
            for s in sems:
                self.wait(eng, s)

    def replay(self, block):
        for eng in ENGINES:
            ops = self.q[eng]
            if not ops:
                continue

            def make(ops):
                def runner(e):
                    for fn in ops:
                        fn(e)
                return runner
            getattr(block, eng)(make(ops))


def build_program():
    nc = bass.Bass("TRN2", target_bir_lowering=False, debug=False)

    def I(name, shape, dt=F32):
        return nc.dram_tensor(name, shape, dt, kind="ExternalInput").ap()

    xT = I("xT", [BL, D, S])
    rpe = I("rpe", [NH, S])
    qkvT = I("qkvT", [D, 3 * NH * DH])
    outwT = I("outwT", [NH * DH, D])
    inWB = I("inWB", [2, 32, 128, 1024])
    convWB = I("convWB", [2, 128, 64])
    convBv = I("convBv", [2, 128, 16])
    xpWB = I("xpWB", [2, 128, 2048])
    dtWT = I("dtWT", [2, DTR, DI])
    dtBv = I("dtBv", [2, 128, 16])
    AlogB = I("AlogB", [2, 128, 256])
    Dpv = I("Dpv", [2, 128, 16])
    moWB = I("moWB", [2, 8, 128, 2048])
    gateB = I("gateB", [128, 64])
    qkvB = I("qkvB", [24, 128, 1024])
    WgB = I("WgB", [32, 128, 1024])
    WuB = I("WuB", [32, 128, 1024])
    WdB = I("WdB", [8, 128, 4096])
    A1cB = I("A1cB", [128, 1024])
    A3cB = I("A3cB", [128, 1024])
    B1c = I("B1c", [128, F])
    B3c = I("B3c", [128, F])
    A2cB = I("A2cB", [128, 4096])
    B2c = I("B2c", [128, D])
    Jrev = I("Jrev", [128, 128])
    Ident = I("Ident", [128, 128])
    Itwo = I("Itwo", [128, 128])
    onesc = I("onesc", [128, 1])
    ones1r = I("ones1r", [1, 128])
    exp16 = I("exp16", [E, 128])
    cextra = I("cextra", [128, 8])

    out_xT = nc.dram_tensor("out_xT", [BL, D, S], F32, kind="ExternalOutput").ap()
    dbg = {}
    if DEBUG == 1:
        for nm, cols in [("lnx", 16384), ("q", 16384), ("k", 16384),
                         ("v", 16896), ("mix", 16384)]:
            dbg[nm] = nc.dram_tensor("dbg_" + nm, [128, cols], BF16,
                                     kind="ExternalOutput").ap()
    if DEBUG == 2:
        dbg["ffn1"] = nc.dram_tensor("dbg_ffn1", [128, 1152], F32,
                                     kind="ExternalOutput").ap()
        for nm, cols in [("ffn2", 16384), ("ffn3", 8192), ("ffn4", 19456)]:
            dbg[nm] = nc.dram_tensor("dbg_" + nm, [128, cols], BF16,
                                     kind="ExternalOutput").ap()
    x_cur = nc.dram_tensor("x_cur", [BL, D, S], F32, kind="Internal").ap()
    g_dram = nc.dram_tensor("g_dram", [NH, 4095], BF16, kind="Internal").ap()
    mm_bc = nc.dram_tensor("mm_bc", [2, 16, 2048], BF16, kind="Internal").ap()
    mm_z = nc.dram_tensor("mm_z", [16, 128, 2048], BF16, kind="Internal").ap()
    mm_yp = nc.dram_tensor("mm_yp", [D, S], F32, kind="Internal").ap()

    ctx = ExitStack()
    with ctx:
        def SB(name, p, cols, dt=F32):
            return ctx.enter_context(nc.sbuf_tensor(name, [p, cols], dt))

        ARN = SB("ARN", 128, ARN_COLS)
        sc1 = SB("sc1", 128, 2048)
        sc2 = SB("sc2", 128, 2048)
        sc3 = SB("sc3", 128, 2048)
        sc4 = SB("sc4", 128, 2048)
        xst = SB("xst", 128, 4096)
        wsl = SB("wsl", 128, 4096, BF16)
        wsl2 = SB("wsl2", 128, 4096, BF16)
        wsm = SB("wsm", 128, 1024, BF16)
        bprob = SB("bprob", 128, 1024, BF16)
        btr = SB("btr", 128, 1024)
        cJ = SB("cJ", 128, 128)
        cI = SB("cI", 128, 128)
        cI2 = SB("cI2", 128, 128)
        cOc = SB("cOc", 128, 1)
        cX = SB("cX", 128, 8)
        cOr = ctx.enter_context(nc.sbuf_tensor("cOr", [1, 128], F32))
        cE16 = ctx.enter_context(nc.sbuf_tensor("cE16", [E, 128], F32))
        cGate = SB("cGate", 128, 64)

        ARNB_t = ARN.ap().bitcast(BF16).tensor
        ARN_t = ARN.ap().tensor
        SC2B_t = sc2.ap().bitcast(BF16).tensor

        PS = [ctx.enter_context(nc.psum_tensor(f"pp{i}", [128, 512], F32)) for i in range(8)]

        block = ctx.enter_context(nc.Block())
        P = Prog(nc, ctx)
        LD, T, V, SS = "ld", "T", "V", "S"
        LG, STQ = "ldg", "st"

        def dma(dst, src, semn=None):
            return P.op("sync", lambda e, d=dst, s=src: e.dma_start(out=d, in_=s),
                        inc=(semn or LD), amt=16)

        def dma_st(dst, src, semn=None):
            return P.op("sync", lambda e, d=dst, s=src: e.dma_start(out=d, in_=s),
                        inc=(semn or STQ), amt=16)

        def dma_cast(dst, src, semn=None):
            return P.op("gpsimd", lambda e, d=dst, s=src: e.dma_start(out=d, in_=s),
                        inc=(semn or LG), amt=16)

        def sl(tens, off, p, w, pstride):
            return AP(tens.ap().tensor, off, [[pstride, p], [1, w]])

        def bv(off, dims, p=128, base=0):
            return AP(ARNB_t, off + base * 2 * ARN_COLS, [[2 * ARN_COLS, p]] + dims)

        def fv(off, dims, p=128, base=0):
            return AP(ARN_t, off + base * ARN_COLS, [[ARN_COLS, p]] + dims)

        dma(cJ.ap(), Jrev)
        dma(cI.ap(), Ident)
        dma(cI2.ap(), Itwo)
        dma_cast(sl(wsm, 0, 128, 128, 1024), Ident)
        dma_cast(sl(wsm, 128, 128, 128, 1024), Jrev)
        dma(cOc.ap(), onesc)
        dma(cX.ap(), cextra)
        dma(cOr.ap(), ones1r)
        c_rpe = dma(sl(sc1, 0, NH, S, 2048), rpe)
        dma(cE16.ap(), exp16)
        c_gate = dma(cGate.ap(), gateB)
        c_xcur = dma(x_cur, xT)
        P.wait("tensor", LD, c_gate)
        for c in range(8):
            P.op("tensor", lambda e, c=c: e.matmul(
                PS[4][0:8, 0:1], sl(cGate, c * 8, 128, 8, 64), cOc[:, :],
                start=(c == 0), stop=(c == 7)), inc=(T if c == 7 else None))
        P.wait("scalar", T, P.c(T))
        P.wait("scalar", LD, P.c(LD))
        P.op("scalar", lambda e: e.activation(
            out=cX[0:8, 4:5], in_=PS[4][0:8, 0:1],
            func=AF.Copy, scale=-1.0), inc=SS)

        P.wait("vector", LD, c_rpe)
        P.op("vector", lambda e: e.memset(sl(sc2, 0, NH, 2047, 2048), PAD))
        P.op("vector", lambda e: e.tensor_tensor_scan(
            out=sl(sc3, 0, NH, S, 2048),
            data0=AP(cOc.ap().tensor, 0, [[1, NH], [0, S]]),
            data1=sl(sc1, 0, NH, S, 2048),
            initial=0.0, op0=ALU.mult, op1=ALU.add), inc=V)
        P.wait("gpsimd", V, P.c(V))
        dma_cast(AP(g_dram.tensor, 0, [[4095, NH], [1, 2047]]), sl(sc2, 0, NH, 2047, 2048))
        dma_cast(AP(g_dram.tensor, 2047, [[4095, NH], [1, 2048]]), sl(sc3, 0, NH, S, 2048))

        def lnx_sl(chunk, t0, w, lnx_off=AT_LNX):
            return AP(ARNB_t, lnx_off + chunk * 2048 + t0, [[2 * ARN_COLS, 128], [1, w]])

        def emit_ln(b, lnx_off=AT_LNX, reverse=False, gate=False):
            P.barrier()
            gmark = 0
            gv_prev = 0
            for tb in range(4):
                t0 = tb * 512
                lds = []
                for c in range(8):
                    lds.append(dma(sl(xst, c * 512, 128, 512, 4096),
                                   AP(x_cur.tensor, (b * D + c * 128) * S + t0,
                                      [[S, 128], [1, 512]])))
                sq_mm_T = {}
                for c in range(8):
                    P.wait("scalar", LD, lds[c])
                    if c >= 4:
                        P.wait("scalar", T, sq_mm_T[c - 4])
                    P.op("scalar", lambda e, c=c: e.activation(
                        out=sl(sc1, (c % 4) * 512, 128, 512, 2048),
                        in_=sl(xst, c * 512, 128, 512, 4096),
                        func=AF.Square), inc=SS)
                    sq_c = P.c(SS)
                    P.wait("tensor", LD, lds[c])
                    P.op("tensor", lambda e, c=c: e.matmul(
                        PS[0][0:1, :], cOc[:, :], sl(xst, c * 512, 128, 512, 4096),
                        start=(c == 0), stop=(c == 7)))
                    if gate:
                        if c == 0 and tb > 0:
                            P.wait("tensor", SS, gmark)
                        P.op("tensor", lambda e, c=c: e.matmul(
                            PS[5][0:8, :], sl(cGate, c * 8, 128, 8, 64),
                            sl(xst, c * 512, 128, 512, 4096),
                            start=(c == 0), stop=(c == 7)))
                    P.wait("tensor", SS, sq_c)
                    sq_mm_T[c] = P.op("tensor", lambda e, c=c: e.matmul(
                        PS[1][0:1, :], cOc[:, :], sl(sc1, (c % 4) * 512, 128, 512, 2048),
                        start=(c == 0), stop=(c == 7)), inc=T)
                P.wait("scalar", T, P.c(T))
                if gate:
                    gmark = P.op("scalar", lambda e, t0=t0: e.activation(
                        out=sl(sc4, t0, 8, 512, 2048),
                        in_=PS[5][0:8, :], func=AF.Copy), inc=SS)
                P.op("scalar", lambda e: e.activation(
                    out=sl(sc2, 0, 1, 512, 2048), in_=PS[0][0:1, :],
                    func=AF.Copy, scale=cX[0:1, 1:2]))
                P.op("scalar", lambda e: e.activation(
                    out=sl(sc2, 512, 1, 512, 2048), in_=PS[1][0:1, :],
                    func=AF.Copy, scale=cX[0:1, 1:2]), inc=SS)
                P.wait("vector", SS, P.c(SS))
                P.op("vector", lambda e: e.tensor_mul(
                    out=sl(sc2, 1024, 1, 512, 2048),
                    in0=sl(sc2, 0, 1, 512, 2048), in1=sl(sc2, 0, 1, 512, 2048)))
                P.op("vector", lambda e: e.tensor_sub(
                    out=sl(sc2, 1024, 1, 512, 2048),
                    in0=sl(sc2, 512, 1, 512, 2048), in1=sl(sc2, 1024, 1, 512, 2048)), inc=V)
                P.wait("scalar", V, P.c(V))
                P.op("scalar", lambda e: e.activation(
                    out=sl(sc2, 1536, 1, 512, 2048), in_=sl(sc2, 1024, 1, 512, 2048),
                    func=AF.Ln, bias=cX[0:1, 0:1]))
                P.op("scalar", lambda e: e.activation(
                    out=sl(sc2, 1024, 1, 512, 2048), in_=sl(sc2, 1536, 1, 512, 2048),
                    func=AF.Exp, scale=cX[0:1, 2:3]), inc=SS)
                P.wait("vector", SS, P.c(SS))
                if gate:
                    P.wait("tensor", SS, P.c(SS))
                    if tb > 0:
                        P.wait("tensor", V, gv_prev)
                    P.op("tensor", lambda e: e.matmul(
                        PS[6][0:8, :], cOr[0:1, 0:8], sl(sc2, 0, 1, 512, 2048),
                        start=True, stop=True))
                    P.op("tensor", lambda e: e.matmul(
                        PS[7][0:8, :], cOr[0:1, 0:8], sl(sc2, 1024, 1, 512, 2048),
                        start=True, stop=True), inc=T)
                    P.wait("vector", T, P.c(T))
                    P.op("vector", lambda e, t0=t0: e.scalar_tensor_tensor(
                        AP(sc2.ap().tensor, 1536, [[2048, 8], [1, 512]]),
                        PS[6][0:8, :], cX[0:8, 4:5],
                        sl(sc4, t0, 8, 512, 2048), ALU.mult, ALU.add), inc=V)
                    P.wait("vector", V, P.c(V))
                    P.op("vector", lambda e, t0=t0: e.tensor_mul(
                        sl(sc4, t0, 8, 512, 2048),
                        AP(sc2.ap().tensor, 1536, [[2048, 8], [1, 512]]),
                        PS[7][0:8, :]), inc=V)
                    gv_prev = P.c(V)
                P.op("vector", lambda e: e.tensor_mul(
                    out=sl(sc2, 0, 1, 512, 2048),
                    in0=sl(sc2, 0, 1, 512, 2048), in1=sl(sc2, 1024, 1, 512, 2048)), inc=V)
                P.wait("tensor", V, P.c(V))
                P.op("tensor", lambda e: e.matmul(
                    PS[2][:, :], cOr[0:1, :], sl(sc2, 1024, 1, 512, 2048),
                    start=True, stop=True))
                P.op("tensor", lambda e: e.matmul(
                    PS[3][:, :], cOr[0:1, :], sl(sc2, 0, 1, 512, 2048),
                    start=True, stop=True), inc=T)
                P.wait("vector", T, P.c(T))
                for c in range(8):
                    P.op("vector", lambda e, c=c: e.tensor_mul(
                        out=sl(sc3, (c % 4) * 512, 128, 512, 2048),
                        in0=sl(xst, c * 512, 128, 512, 4096), in1=PS[2][:, :]))
                    if not reverse:
                        oap = AP(ARNB_t, lnx_off + c * 2048 + t0, [[2 * ARN_COLS, 128], [1, 512]])
                    else:
                        oap = AP(ARNB_t, lnx_off + c * 2048 + (S - 1 - t0),
                                 [[2 * ARN_COLS, 128], [-1, 512]])
                    P.op("vector", lambda e, c=c, oap=oap: e.tensor_sub(
                        out=oap,
                        in0=sl(sc3, (c % 4) * 512, 128, 512, 2048),
                        in1=PS[3][:, :]), inc=(V if c == 7 else None))
                P.wait("tensor", V, P.c(V))
                P.wait("scalar", V, P.c(V))
                if tb < 3:
                    P.wait("sync", V, P.c(V))
            return P.c(V)

        def dump(nm, off, cols):
            if not DEBUG or nm not in dbg:
                return
            P.barrier()
            dma_st(dbg[nm], AP(ARNB_t, off, [[2 * ARN_COLS, 128], [1, cols]]))
            P.wait("sync", STQ, P.c(STQ))

        def emit_attn(b):
            emit_ln(b, AT_LNX)
            ln_done = P.c(V)
            if b == 0:
                dump("lnx", AT_LNX, 16384)

            for kind in range(2):
                dstoff = AT_Q if kind == 0 else AT_K
                for p in range(8):
                    mcol = kind * 1024 + p * 128
                    slab = wsl if p % 2 == 0 else wsl2
                    sgn = "lgA" if p % 2 == 0 else "lgB"
                    cw = dma_cast(sl(slab, 0, 128, 1024, 4096),
                                  AP(qkvB.tensor, (kind * 8 + p) * 128 * 1024,
                                     [[1024, 128], [1, 1024]]), semn=sgn)
                    P.wait("tensor", sgn, cw)
                    P.wait("tensor", V, ln_done)
                    for sub in range(4):
                        bank = 4 + (sub % 2)
                        for c in range(8):
                            P.op("tensor", lambda e, c=c, sub=sub, slab=slab, bank=bank: e.matmul(
                                PS[bank][:, :], sl(slab, c * 128, 128, 128, 4096),
                                lnx_sl(c, sub * 512, 512),
                                start=(c == 0), stop=(c == 7)), inc=(T if c == 7 else None))
                        P.wait("scalar", T, P.c(T))
                        P.op("scalar", lambda e, p=p, sub=sub, dstoff=dstoff, kind=kind, bank=bank: e.activation(
                            out=bv(dstoff + p * 2048 + sub * 512, [[1, 512]]),
                            in_=PS[bank][:, :], func=AF.Copy,
                            scale=(cX[:, 3:4] if kind == 0 else 1.0)), inc=SS)
                        P.wait("tensor", SS, P.c(SS) - 1)
                    P.wait("tensor", SS, P.c(SS))
                    P.wait("gpsimd", T, P.c(T))

            if b == 0:
                dump("q", AT_Q, 16384)
                dump("k", AT_K, 16384)
            P.op("vector", lambda e: e.memset(
                AP(ARNB_t, AT_V + 64, [[2 * ARN_COLS, 128], [66, 256], [1, 1]]), 1.0), inc=V)
            P.wait("gpsimd", T)
            cw = None
            for c in range(8):
                dma_cast(sl(wsl, c * 512, 128, 512, 4096),
                         AP(qkvT.tensor, c * 128 * 3072 + 2048, [[3072, 128], [1, 512]]))
                cw = dma_cast(sl(wsl2, c * 512, 128, 512, 4096),
                              AP(qkvT.tensor, c * 128 * 3072 + 2048 + 512,
                                 [[3072, 128], [1, 512]]))
            P.wait("tensor", LG, cw)
            P.wait("tensor", V, P.c(V))
            for j in range(16):
                for half in range(2):
                    bank = 6 + half
                    slab = wsl if half == 0 else wsl2
                    for c in range(8):
                        P.op("tensor", lambda e, j=j, c=c, slab=slab, bank=bank: e.matmul(
                            PS[bank][:, :], lnx_sl(c, j * 128, 128),
                            sl(slab, c * 512, 128, 512, 4096),
                            start=(c == 0), stop=(c == 7)), inc=(T if c == 7 else None))
                P.wait("scalar", T, P.c(T))
                for half in range(2):
                    P.op("scalar", lambda e, j=j, half=half: e.activation(
                        out=AP(ARNB_t, AT_V + j * (16 * 66) + half * 8 * 66,
                               [[2 * ARN_COLS, 128], [66, 8], [1, 64]]),
                        in_=PS[6 + half][:, :], func=AF.Copy), inc=SS)
                P.wait("tensor", SS, P.c(SS))
            if b == 0:
                dump("v", AT_V, 16896)
            P.barrier(engs=("vector", "scalar"))

            last_qk_T = {}
            last_exp_S = {}
            last_pv_T = {}
            P.wait("tensor", SS, P.c(SS))
            BTRB2_t = btr.ap().bitcast(BF16).tensor

            def prob_slot(buf):
                if buf < 2:
                    return sl(bprob, buf * 512, 128, 512, 1024)
                return AP(XSTB_t, (buf - 2) * 512, [[2 * 4096, 128], [1, 512]])

            for n in range(16):
                p, hb = n // 2, (n % 2) * 64
                for sqb in range(4):
                    i0 = sqb * 512
                    nsk = sqb * 4 + 4
                    for jt in range(nsk):
                        j0 = jt * 128
                        buf = jt % 4
                        qeng = "gpsimd" if buf % 2 == 0 else "sync"
                        if buf in last_qk_T:
                            P.wait(qeng, T, last_qk_T[buf])
                        c_t = P.op(qeng, lambda e, buf=buf, src=AP(
                            g_dram.tensor, n * 4095 + 1920 - j0 + i0,
                            [[1, 128], [1, 512]]): e.dma_start(
                            out=AP(BTRB2_t, buf * 512, [[2048, 128], [1, 512]]), in_=src),
                            inc="lt" + str(buf), amt=16)
                        P.wait("tensor", "lt" + str(buf), c_t)
                        if buf in last_exp_S:
                            P.wait("tensor", SS, last_exp_S[buf])
                        P.op("tensor", lambda e, buf=buf: e.matmul(
                            PS[4 + buf][:, :], sl(wsm, 128, 128, 128, 1024),
                            AP(BTRB2_t, buf * 512, [[2048, 128], [1, 512]]),
                            start=True, stop=False))
                        last_qk_T[buf] = P.op("tensor", lambda e, hb=hb, p=p, j0=j0, i0=i0, buf=buf: e.matmul(
                            PS[4 + buf][:, :],
                            AP(ARNB_t, AT_K + hb * 2 * ARN_COLS + p * 2048 + j0,
                               [[2 * ARN_COLS, 64], [1, 128]]),
                            AP(ARNB_t, AT_Q + hb * 2 * ARN_COLS + p * 2048 + i0,
                               [[2 * ARN_COLS, 64], [1, 512]]),
                            start=False, stop=True), inc=T)
                        P.wait("scalar", T, last_qk_T[buf])
                        if buf in last_pv_T:
                            P.wait("scalar", T, last_pv_T[buf])
                        last_exp_S[buf] = P.op("scalar", lambda e, buf=buf: e.activation(
                            out=prob_slot(buf),
                            in_=PS[4 + buf][:, :], func=AF.Exp), inc=SS)
                        P.wait("tensor", SS, last_exp_S[buf])
                        last_pv_T[buf] = P.op("tensor", lambda e, n=n, jt=jt, buf=buf, nsk=nsk: e.matmul(
                            PS[2][0:65, :],
                            AP(ARNB_t, AT_V + jt * (16 * 66) + n * 66,
                               [[2 * ARN_COLS, 128], [1, 65]]),
                            prob_slot(buf),
                            start=(jt == 0), stop=(jt == nsk - 1)), inc=T)
                    P.wait("scalar", T, P.c(T))
                    P.op("scalar", lambda e: e.activation(
                        out=sl(sc4, 0, 1, 512, 2048), in_=PS[2][64:65, :],
                        func=AF.Ln), inc=SS)
                    P.wait("scalar", SS, P.c(SS))
                    P.op("scalar", lambda e: e.activation(
                        out=sl(sc4, 0, 1, 512, 2048), in_=sl(sc4, 0, 1, 512, 2048),
                        func=AF.Exp, scale=-1.0), inc=SS)
                    P.wait("tensor", SS, P.c(SS))
                    P.op("tensor", lambda e: e.matmul(
                        PS[3][0:64, :], cOr[0:1, 0:64], sl(sc4, 0, 1, 512, 2048),
                        start=True, stop=True), inc=T)
                    P.wait("scalar", T, P.c(T))
                    P.op("scalar", lambda e: e.activation(
                        out=sl(sc4, 512, 64, 512, 2048), in_=PS[3][0:64, :],
                        func=AF.Copy), inc=SS)
                    P.wait("vector", SS, P.c(SS))
                    P.op("vector", lambda e, hb=hb, p=p, i0=i0: e.tensor_mul(
                        out=AP(ARNB_t, AT_LNX + hb * 2 * ARN_COLS + p * 2048 + i0,
                               [[2 * ARN_COLS, 64], [1, 512]]),
                        in0=PS[2][0:64, :], in1=sl(sc4, 512, 64, 512, 2048)), inc=V)
                    P.wait("tensor", V, P.c(V))

            if b == 0:
                dump("mix", AT_LNX, 16384)
            P.wait("gpsimd", T)
            P.wait("sync", T, P.c(T))
            cw = None
            for k in range(8):
                slab, col = (wsl, (k % 4) * 1024) if k < 4 else (wsl2, (k % 4) * 1024)
                for hh in range(2):
                    cw = dma_cast(sl(slab, col + hh * 512, 128, 512, 4096),
                                  AP(outwT.tensor, k * 128 * D + hh * 512,
                                     [[D, 128], [1, 512]]))
            P.wait("tensor", LG, cw)
            for dtile in range(8):
                for sub in range(4):
                    t0 = sub * 512
                    bank = 4 + (sub % 2)
                    c_x = dma(sl(xst, (dtile % 2) * 2048 + sub * 512, 128, 512, 4096),
                              AP(x_cur.tensor, (b * D + dtile * 128) * S + t0,
                                 [[S, 128], [1, 512]]))
                    for k in range(8):
                        slab, col = (wsl, (k % 4) * 1024) if k < 4 else (wsl2, (k % 4) * 1024)
                        P.op("tensor", lambda e, k=k, dtile=dtile, t0=t0, slab=slab, col=col, bank=bank: e.matmul(
                            PS[bank][:, :],
                            sl(slab, col + dtile * 128, 128, 128, 4096),
                            AP(ARNB_t, AT_LNX + k * 2048 + t0, [[2 * ARN_COLS, 128], [1, 512]]),
                            start=(k == 0), stop=(k == 7)), inc=(T if k == 7 else None))
                    P.wait("vector", T, P.c(T))
                    P.wait("vector", LD, c_x)
                    P.op("vector", lambda e, dtile=dtile, sub=sub, bank=bank: e.tensor_add(
                        out=sl(sc3, (sub % 2) * 512, 128, 512, 2048),
                        in0=sl(xst, (dtile % 2) * 2048 + sub * 512, 128, 512, 4096),
                        in1=PS[bank][:, :]), inc=V)
                    P.wait("sync", V, P.c(V))
                    dma(AP(x_cur.tensor, (b * D + dtile * 128) * S + t0, [[S, 128], [1, 512]]),
                        sl(sc3, (sub % 2) * 512, 128, 512, 2048))
                    P.wait("tensor", V, P.c(V))

        XSTB_t = xst.ap().bitcast(BF16).tensor

        def xbv(off, dims, p=128):
            return AP(XSTB_t, off, [[2 * 4096, p]] + dims)

        def s3(off, dims, p=128):
            return AP(sc3.ap().tensor, off, [[2048, p]] + dims)

        def emit_ffn(b):
            emit_ln(b, AT_LNX, gate=True)
            P.wait("gpsimd", T)
            dma_cast(sl(wsl, 0, 128, 4096, 4096), B1c)
            dma_cast(sl(wsl2, 0, 128, 4096, 4096), B3c)
            dma_cast(bv(FF_A2C, [[1, 4096]]), A2cB)
            dma_cast(bv(FF_B2C, [[1, 1024]]), B2c)
            dma_cast(bv(FF_A1CT, [[1, 1024]]), A1cB)
            dma_cast(bv(FF_A3CT, [[1, 1024]]), A3cB)
            c_slab = P.c(LG)

            ln_done = P.c(V)
            P.wait("tensor", V, ln_done)
            P.wait("tensor", SS, P.c(SS))
            for ct in range(16):
                P.op("tensor", lambda e, ct=ct: e.transpose(
                    out=PS[1][:, ct * 8:(ct + 1) * 8],
                    in_=sl(sc4, ct * 128, 8, 128, 2048),
                    identity=cI[0:8, 0:8]), inc=(T if ct == 15 else None))
            P.wait("vector", T, P.c(T))

            def vstep(fn):
                P.op("vector", fn, inc=V)
                P.wait("vector", V, P.c(V))

            vstep(lambda e: e.tensor_copy(s3(0, [[1, 128]]), PS[1][:, 0:128]))
            vstep(lambda e: e.tensor_tensor(
                s3(1024, [[4, 16], [1, 4]]),
                s3(0, [[8, 16], [1, 4]]), s3(4, [[8, 16], [1, 4]]), ALU.max))
            vstep(lambda e: e.tensor_tensor(
                s3(1088, [[2, 16], [1, 2]]),
                s3(1024, [[4, 16], [1, 2]]), s3(1026, [[4, 16], [1, 2]]), ALU.max))
            vstep(lambda e: e.tensor_tensor(
                s3(512, [[1, 16]]),
                s3(1088, [[2, 16], [1, 1]]), s3(1089, [[2, 16], [1, 1]]), ALU.max))
            vstep(lambda e: e.tensor_tensor(
                s3(128, [[1, 128]]), s3(0, [[1, 128]]),
                s3(512, [[1, 16], [0, 8]]), ALU.is_equal))
            vstep(lambda e: e.scalar_tensor_tensor(
                s3(384, [[1, 128]]), s3(128, [[1, 128]]), -1e9,
                s3(0, [[1, 128]]), ALU.mult, ALU.add))
            vstep(lambda e: e.tensor_tensor(
                s3(1024, [[4, 16], [1, 4]]),
                s3(384, [[8, 16], [1, 4]]), s3(388, [[8, 16], [1, 4]]), ALU.max))
            vstep(lambda e: e.tensor_tensor(
                s3(1088, [[2, 16], [1, 2]]),
                s3(1024, [[4, 16], [1, 2]]), s3(1026, [[4, 16], [1, 2]]), ALU.max))
            vstep(lambda e: e.tensor_tensor(
                s3(528, [[1, 16]]),
                s3(1088, [[2, 16], [1, 1]]), s3(1089, [[2, 16], [1, 1]]), ALU.max))
            vstep(lambda e: e.tensor_tensor(
                s3(256, [[1, 128]]), s3(384, [[1, 128]]),
                s3(528, [[1, 16], [0, 8]]), ALU.is_equal))
            vstep(lambda e: e.tensor_sub(
                s3(832, [[1, 16]]), s3(512, [[1, 16]]), s3(528, [[1, 16]])))
            P.wait("scalar", V, P.c(V))
            P.op("scalar", lambda e: e.activation(
                out=s3(544, [[1, 16]]), in_=s3(832, [[1, 16]]), func=AF.Sigmoid), inc=SS)
            P.wait("scalar", SS, P.c(SS))
            P.op("scalar", lambda e: e.activation(
                out=s3(560, [[1, 16]]), in_=s3(832, [[1, 16]]), func=AF.Sigmoid,
                scale=-1.0), inc=SS)
            P.wait("vector", SS, P.c(SS))
            vstep(lambda e: e.tensor_mul(
                s3(576, [[1, 128]]), s3(128, [[1, 128]]), s3(544, [[1, 16], [0, 8]])))
            vstep(lambda e: e.tensor_mul(
                s3(704, [[1, 128]]), s3(256, [[1, 128]]), s3(560, [[1, 16], [0, 8]])))
            exps = [(128, 0, FF_M1X, cI2), (256, 0, FF_M1X + 2048, cI2),
                    (576, 0, FF_M1X + 4096, cI2), (704, 0, FF_M1X + 6144, cI2),
                    (544, 1, FF_W1B, cI), (560, 1, FF_W1B + 2048, cI)]
            P.wait("vector", V, P.c(V))
            for src, w16, dst, rhs in exps:
                for g in range(4):
                    P.wait("vector", T, P.c(T))
                    for j in range(4):
                        ct = g * 4 + j
                        if w16:
                            iap = s3(src + ct, [[0, 128]])
                        else:
                            iap = s3(src + ct * 8, [[1, 8], [0, 16]])
                        P.op("vector", lambda e, iap=iap, j=j: e.tensor_copy(
                            AP(sc2.ap().tensor, j * 128, [[2048, 128], [1, 128]]), iap), inc=V)
                        P.wait("tensor", V, P.c(V))
                        if P.c(SS) > 0:
                            P.wait("tensor", SS, P.c(SS))
                        P.op("tensor", lambda e, j=j, rhs=rhs: e.matmul(
                            PS[2][:, j * 128:(j + 1) * 128],
                            AP(sc2.ap().tensor, j * 128, [[2048, 128], [1, 128]]),
                            rhs[:, :], start=True, stop=True), inc=(T if j == 3 else None))
                    P.wait("scalar", T, P.c(T))
                    P.op("scalar", lambda e, dst=dst, g=g: e.activation(
                        out=bv(dst + g * 512, [[1, 512]]), in_=PS[2][:, :],
                        func=AF.Copy), inc=SS)
            P.wait("tensor", SS, P.c(SS))
            P.wait("vector", SS, P.c(SS))
            if DEBUG == 2 and b == 0:
                P.barrier()
                dma_st(dbg["ffn1"], sl(sc3, 0, 128, 1152, 2048))
                dma_st(dbg["ffn3"], AP(ARNB_t, 32768, [[2 * ARN_COLS, 128], [1, 8192]]))
                P.wait("sync", STQ, P.c(STQ))

            for tq in range(4):
                for c in range(8):
                    P.op("tensor", lambda e, c=c, tq=tq: e.matmul(
                        PS[0][:, :], bv(FF_A1CT + c * 128, [[1, 128]]),
                        lnx_sl(c, tq * 512, 512), start=(c == 0), stop=(c == 7)))
                    P.op("tensor", lambda e, c=c, tq=tq: e.matmul(
                        PS[1][:, :], bv(FF_A3CT + c * 128, [[1, 128]]),
                        lnx_sl(c, tq * 512, 512), start=(c == 0), stop=(c == 7)),
                        inc=(T if c == 7 else None))
                P.wait("scalar", T, P.c(T))
                P.op("scalar", lambda e, tq=tq: e.activation(
                    out=bv(FF_XA1 + tq * 512, [[1, 512]]), in_=PS[0][:, :], func=AF.Copy))
                P.op("scalar", lambda e, tq=tq: e.activation(
                    out=bv(FF_XA3 + tq * 512, [[1, 512]]), in_=PS[1][:, :], func=AF.Copy), inc=SS)
                P.wait("tensor", SS, P.c(SS))
            P.wait("vector", SS, P.c(SS))
            for i, (xa, msk) in enumerate([(FF_XA1, FF_M1X), (FF_XA3, FF_M1X),
                                           (FF_XA1, FF_M1X + 2048), (FF_XA3, FF_M1X + 2048)]):
                P.op("vector", lambda e, xa=xa, msk=msk, i=i: e.tensor_mul(
                    bv(FF_XM + i * 2048, [[1, 2048]]),
                    bv(xa, [[1, 2048]]), bv(msk, [[1, 2048]])), inc=V)

            if DEBUG == 2 and b == 0:
                P.barrier()
                dma_st(dbg["ffn2"], AP(ARNB_t, 16384, [[2 * ARN_COLS, 128], [1, 16384]]))
                P.wait("sync", STQ, P.c(STQ))

            TMP_A1, TMP_A2, TMP_S1, TMP_T3 = FF_A1, FF_A1 + 512, FF_A1 + 1024, FF_A1 + 1536
            xam_off = [FF_XM, FF_XM + 2048, FF_XM + 4096, FF_XM + 6144]
            for tq in range(4):
                t0 = tq * 512
                P.wait("gpsimd", T, P.c(T))
                P.wait("gpsimd", V, P.c(V))
                tmark = {}
                vmark = {}
                smark = {}
                for fc in range(32):
                    buf = fc % 2
                    if fc >= 2:
                        P.wait("gpsimd", T, tmark[fc - 2])
                    sgn = "lgA" if buf == 0 else "lgB"
                    dma_cast(xbv(buf * 1024, [[1, 1024]]),
                             AP(WgB.tensor, fc * 128 * 1024, [[1024, 128], [1, 1024]]), semn=sgn)
                    cw = dma_cast(xbv(2048 + buf * 1024, [[1, 1024]]),
                                  AP(WuB.tensor, fc * 128 * 1024, [[1024, 128], [1, 1024]]), semn=sgn)
                    P.wait("tensor", sgn, cw)
                    if fc >= 1:
                        P.wait("tensor", V, vmark[fc - 1])
                        P.wait("tensor", SS, smark[fc - 1])
                    else:
                        P.wait("tensor", V, P.c(V))
                        P.wait("tensor", SS, P.c(SS))
                    for c in range(8):
                        P.op("tensor", lambda e, c=c, buf=buf, t0=t0: e.matmul(
                            PS[0][:, :], xbv(buf * 1024 + c * 128, [[1, 128]]),
                            lnx_sl(c, t0, 512), start=(c == 0), stop=(c == 7)))
                        P.op("tensor", lambda e, c=c, buf=buf, t0=t0: e.matmul(
                            PS[1][:, :], xbv(2048 + buf * 1024 + c * 128, [[1, 128]]),
                            lnx_sl(c, t0, 512), start=(c == 0), stop=(c == 7)))
                    P.op("tensor", lambda e, fc=fc, t0=t0: e.matmul(
                        PS[2][:, :], sl(wsl, fc * 128, 128, 128, 4096),
                        bv(xam_off[0] + t0, [[1, 512]]), start=True, stop=True))
                    P.op("tensor", lambda e, fc=fc, t0=t0: e.matmul(
                        PS[3][:, :], sl(wsl, fc * 128, 128, 128, 4096),
                        bv(xam_off[2] + t0, [[1, 512]]), start=True, stop=True))
                    P.op("tensor", lambda e, fc=fc, t0=t0: e.matmul(
                        PS[4][:, :], sl(wsl2, fc * 128, 128, 128, 4096),
                        bv(xam_off[1] + t0, [[1, 512]]), start=True, stop=True))
                    tmark[fc] = P.op("tensor", lambda e, fc=fc, t0=t0: e.matmul(
                        PS[5][:, :], sl(wsl2, fc * 128, 128, 128, 4096),
                        bv(xam_off[3] + t0, [[1, 512]]), start=True, stop=True), inc=T)
                    P.wait("scalar", T, tmark[fc])
                    if fc >= 1:
                        P.wait("scalar", V, vmark[fc - 1])
                    P.op("scalar", lambda e: e.activation(
                        out=sl(sc1, 0, 128, 512, 2048), in_=PS[0][:, :], func=AF.Copy))
                    sm = P.op("scalar", lambda e: e.activation(
                        out=sl(sc1, 512, 128, 512, 2048), in_=PS[1][:, :], func=AF.Copy), inc=SS)
                    for slot in range(2):
                        d1b, d3b = (2, 4) if slot == 0 else (3, 5)
                        tmpa = TMP_A1 if slot == 0 else TMP_A2
                        P.wait("vector", SS, sm)
                        P.wait("vector", T, tmark[fc])
                        P.op("vector", lambda e, d1b=d1b: e.tensor_add(
                            sl(sc2, 512, 128, 512, 2048), PS[d1b][:, :],
                            sl(sc1, 0, 128, 512, 2048)), inc=V)
                        P.wait("scalar", V, P.c(V))
                        sl_s1 = P.op("scalar", lambda e: e.activation(
                            out=bv(TMP_S1, [[1, 512]]), in_=sl(sc2, 512, 128, 512, 2048),
                            func=AF.Silu), inc=SS)
                        P.op("vector", lambda e, d3b=d3b: e.tensor_add(
                            bv(TMP_T3, [[1, 512]]), PS[d3b][:, :],
                            sl(sc1, 512, 128, 512, 2048)))
                        P.wait("vector", SS, sl_s1)
                        P.op("vector", lambda e, tmpa=tmpa: e.tensor_mul(
                            bv(tmpa, [[1, 512]]), bv(TMP_S1, [[1, 512]]),
                            bv(TMP_T3, [[1, 512]])), inc=V)
                    smark[fc] = P.c(SS)
                    P.op("vector", lambda e, fc=fc, t0=t0: e.tensor_mul(
                        bv(FF_WACT + fc * 512, [[1, 512]]),
                        bv(TMP_A1, [[1, 512]]), bv(FF_W1B + t0, [[1, 512]])))
                    P.op("vector", lambda e, t0=t0: e.tensor_mul(
                        bv(TMP_T3, [[1, 512]]),
                        bv(TMP_A2, [[1, 512]]), bv(FF_W1B + 2048 + t0, [[1, 512]])))
                    P.op("vector", lambda e, fc=fc: e.tensor_add(
                        bv(FF_WACT + fc * 512, [[1, 512]]),
                        bv(FF_WACT + fc * 512, [[1, 512]]), bv(TMP_T3, [[1, 512]])), inc=V)
                    vmark[fc] = P.c(V)
                    P.wait("tensor", V, vmark[fc])
                    if fc == 0:
                        P.wait("tensor", V, P.c(V))
                    P.op("tensor", lambda e, fc=fc: e.matmul(
                        PS[6][:, :], bv(FF_A2C + fc * 128, [[1, 128]]),
                        bv(TMP_A1, [[1, 512]]), start=(fc == 0), stop=(fc == 31)))
                    zzm = P.op("tensor", lambda e, fc=fc: e.matmul(
                        PS[7][:, :], bv(FF_A2C + fc * 128, [[1, 128]]),
                        bv(TMP_A2, [[1, 512]]), start=(fc == 0), stop=(fc == 31)),
                        inc=(T if fc == 31 else None))
                P.wait("vector", T, P.c(T))
                P.op("vector", lambda e, t0=t0: e.tensor_mul(
                    bv(FF_ZW, [[1, 512]]), PS[6][:, :], bv(FF_M1X + 4096 + t0, [[1, 512]])))
                P.op("vector", lambda e, t0=t0: e.tensor_mul(
                    bv(FF_ZW + 512, [[1, 512]]), PS[7][:, :],
                    bv(FF_M1X + 6144 + t0, [[1, 512]])), inc=V)
                zw_done = P.c(V)
                if DEBUG == 2 and b == 0 and tq == 2:
                    P.barrier()
                    dma_st(dbg["ffn4"], AP(ARNB_t, 40960, [[2 * ARN_COLS, 128], [1, 19456]]))
                    P.wait("sync", STQ, P.c(STQ))
                dmark = {}
                xmark = {}
                stmark = {}
                for dc in range(8):
                    pb = dc % 2
                    if dc >= 1:
                        P.wait("gpsimd", T, dmark[dc - 1])
                    cw = dma_cast(xbv(4096, [[1, 4096]]),
                                  AP(WdB.tensor, dc * 128 * 4096, [[4096, 128], [1, 4096]]),
                                  semn="lgW")
                    if dc == 0:
                        P.wait("sync", SS, P.c(SS))
                        P.wait("sync", V, P.c(V))
                    if dc >= 2:
                        P.wait("sync", V, xmark[dc - 2])
                    c_x = dma(sl(sc2, (dc % 2) * 512, 128, 512, 2048),
                              AP(x_cur.tensor, (b * D + dc * 128) * S + t0,
                                 [[S, 128], [1, 512]]), semn="lx" + str(dc % 2))
                    P.wait("tensor", "lgW", cw)
                    P.wait("tensor", V, zw_done)
                    if dc == 0:
                        P.wait("tensor", SS, P.c(SS))
                    if dc >= 2:
                        P.wait("tensor", V, xmark[dc - 2])
                    for fc in range(32):
                        P.op("tensor", lambda e, fc=fc, pb=pb: e.matmul(
                            PS[pb][:, :], xbv(4096 + fc * 128, [[1, 128]]),
                            bv(FF_WACT + fc * 512, [[1, 512]]),
                            start=(fc == 0), stop=False))
                    P.op("tensor", lambda e, dc=dc, pb=pb: e.matmul(
                        PS[pb][:, :], bv(FF_B2C + dc * 128, [[1, 128]]),
                        bv(FF_ZW, [[1, 512]]), start=False, stop=False))
                    dmark[dc] = P.op("tensor", lambda e, dc=dc, pb=pb: e.matmul(
                        PS[pb][:, :], bv(FF_B2C + dc * 128, [[1, 128]]),
                        bv(FF_ZW + 512, [[1, 512]]), start=False, stop=True), inc=T)
                    P.wait("vector", T, dmark[dc])
                    P.wait("vector", "lx" + str(dc % 2), c_x)
                    if dc >= 2:
                        P.wait("vector", STQ, stmark[dc - 2])
                    P.op("vector", lambda e, dc=dc, pb=pb: e.tensor_add(
                        sl(sc2, 1024 + (dc % 2) * 512, 128, 512, 2048),
                        PS[pb][:, :], sl(sc2, (dc % 2) * 512, 128, 512, 2048)), inc=V)
                    xmark[dc] = P.c(V)
                    P.wait("sync", V, xmark[dc])
                    dma_st(AP(x_cur.tensor, (b * D + dc * 128) * S + t0, [[S, 128], [1, 512]]),
                           sl(sc2, 1024 + (dc % 2) * 512, 128, 512, 2048))
                    stmark[dc] = P.c(STQ)

        MM_U = 0
        MM_XI = 32768
        MM_B = 32768
        MM_C = 49152
        MM_XD = 65600
        SC4B_t = sc4.ap().bitcast(BF16).tensor
        SC3B_t = sc3.ap().bitcast(BF16).tensor
        BTRB_t = btr.ap().bitcast(BF16).tensor

        def arn_b(off, w, p=128):
            return AP(ARNB_t, off, [[2 * ARN_COLS, p], [1, w]])

        def emit_mamba(b):
            for dr in range(2):
                emit_ln(b, AT_LNX)
                ln_done = P.c(V)
                c_al = dma(sl(sc3, 0, 128, 256, 2048),
                           AP(AlogB.tensor, dr * 128 * 256, [[256, 128], [1, 256]]))
                dma(sl(sc1, 256, 128, 64, 2048),
                    AP(convWB.tensor, dr * 128 * 64, [[64, 128], [1, 64]]))
                dma(sl(sc1, 320, 128, 16, 2048),
                    AP(convBv.tensor, dr * 128 * 16, [[16, 128], [1, 16]]))
                dma(sl(sc1, 336, 128, 16, 2048),
                    AP(dtBv.tensor, dr * 128 * 16, [[16, 128], [1, 16]]))
                c_cn = dma(sl(sc1, 352, 128, 16, 2048),
                           AP(Dpv.tensor, dr * 128 * 16, [[16, 128], [1, 16]]))
                P.wait("scalar", LD, c_al)
                P.op("scalar", lambda e: e.activation(
                    out=sl(sc3, 256, 128, 256, 2048), in_=sl(sc3, 0, 128, 256, 2048),
                    func=AF.Exp), inc=SS)
                P.wait("vector", SS, P.c(SS))
                P.op("vector", lambda e: e.tensor_scalar(
                    sl(sc1, 0, 128, 256, 2048), sl(sc3, 256, 128, 256, 2048),
                    -1.0, None, ALU.mult), inc=V)
                P.op("vector", lambda e: e.memset(
                    AP(ARNB_t, MM_XI, [[2 * ARN_COLS, 128], [2052, 16], [1, 3]]), 0.0), inc=V)

                m1_t = {}
                m1_s = {}
                zst = {}
                for ec in range(32):
                    buf = ec % 2
                    if ec >= 2:
                        P.wait("gpsimd", T, m1_t[ec - 2])
                    sgn = "lgA" if buf == 0 else "lgB"
                    cw = dma_cast(sl(wsl, buf * 1024, 128, 1024, 4096),
                                  AP(inWB.tensor, (dr * 32 + ec) * 128 * 1024,
                                     [[1024, 128], [1, 1024]]), semn=sgn)
                    P.wait("tensor", sgn, cw)
                    P.wait("tensor", V, ln_done)
                    if ec >= 1:
                        P.wait("tensor", SS, m1_s[ec - 1])
                    if ec == 0:
                        P.wait("tensor", SS, P.c(SS))
                    for tq in range(4):
                        t0 = tq * 512
                        for c in range(8):
                            if dr == 0:
                                rap = AP(ARNB_t, c * 2048 + t0, [[2 * ARN_COLS, 128], [1, 512]])
                            else:
                                rap = AP(ARNB_t, c * 2048 + (2047 - t0), [[2 * ARN_COLS, 128], [-1, 512]])
                            P.op("tensor", lambda e, tq=tq, c=c, buf=buf, rap=rap: e.matmul(
                                PS[tq][:, :], sl(wsl, buf * 1024 + c * 128, 128, 128, 4096),
                                rap, start=(c == 0), stop=(c == 7)),
                                inc=(T if (tq == 3 and c == 7) else None))
                    m1_t[ec] = P.c(T)
                    P.wait("scalar", T, m1_t[ec])
                    if ec >= 18:
                        P.wait("scalar", "sz" + str(ec % 2), zst[ec - 2])
                    for tq in range(4):
                        if ec < 16:
                            P.op("scalar", lambda e, ec=ec, tq=tq: e.activation(
                                out=arn_b(MM_XI + ec * 2052 + 3 + tq * 512, 512),
                                in_=PS[tq][:, :], func=AF.Copy),
                                inc=(SS if tq == 3 else None))
                        else:
                            P.op("scalar", lambda e, ec=ec, tq=tq: e.activation(
                                out=AP(SC4B_t, (ec % 2) * 2048 + tq * 512, [[4096, 128], [1, 512]]),
                                in_=PS[tq][:, :], func=AF.Silu),
                                inc=(SS if tq == 3 else None))
                    m1_s[ec] = P.c(SS)
                    if ec >= 16:
                        P.wait("sync", SS, m1_s[ec])
                        zst[ec] = dma_st(
                            AP(mm_z.tensor, (ec - 16) * 128 * 2048, [[2048, 128], [1, 2048]]),
                            AP(SC4B_t, (ec % 2) * 2048, [[4096, 128], [1, 2048]]), semn="sz" + str(ec % 2))

                P.barrier()
                cvm = {}
                for c in range(16):
                    base = MM_XI + c * 2052
                    if c >= 1:
                        P.wait("vector", SS, cvm[c - 1])
                    P.op("vector", lambda e, base=base, c=c: e.tensor_scalar(
                        sl(sc3, 0, 128, 2048, 2048), arn_b(base + 0, 2048),
                        sl(sc1, 256 + c * 4 + 0, 128, 1, 2048), None, ALU.mult))
                    for j in range(1, 4):
                        P.op("vector", lambda e, base=base, c=c, j=j: e.scalar_tensor_tensor(
                            sl(sc3, 0, 128, 2048, 2048), arn_b(base + j, 2048),
                            sl(sc1, 256 + c * 4 + j, 128, 1, 2048),
                            sl(sc3, 0, 128, 2048, 2048), ALU.mult, ALU.add),
                            inc=(V if j == 3 else None))
                    P.wait("scalar", V, P.c(V))
                    cvm[c] = P.op("scalar", lambda e, c=c: e.activation(
                        out=arn_b(MM_U + c * 2048, 2048), in_=sl(sc3, 0, 128, 2048, 2048),
                        func=AF.Silu, bias=sl(sc1, 320 + c, 128, 1, 2048)), inc=SS)

                P.wait("gpsimd", T, P.c(T))
                cw = dma_cast(sl(wsl2, 0, 128, 2048, 4096),
                              AP(xpWB.tensor, dr * 128 * 2048, [[2048, 128], [1, 2048]]))
                dma_cast(AP(wsl.ap().tensor, 64 * 4096, [[4096, 64], [1, 2048]]),
                         AP(dtWT.tensor, dr * 64 * 2048, [[2048, 64], [1, 2048]]))
                cw_dtw = P.c(LG)
                P.wait("tensor", LG, cw)
                P.wait("tensor", SS, P.c(SS))
                for q in range(4):
                    for c in range(16):
                        P.op("tensor", lambda e, q=q, c=c: e.matmul(
                            PS[q][:, :], sl(wsl2, c * 128, 128, 128, 4096),
                            arn_b(MM_U + c * 2048 + q * 512, 512),
                            start=(c == 0), stop=(c == 15)), inc=(T if c == 15 else None))
                P.wait("scalar", T, P.c(T))
                for q in range(4):
                    P.op("scalar", lambda e, q=q: e.activation(
                        out=arn_b(MM_XD + q * 512, 512), in_=PS[q][:, :], func=AF.Copy),
                        inc=(SS if q == 3 else None))
                P.wait("sync", SS, P.c(SS))
                dma_st(AP(mm_bc.tensor, 0, [[2048, 16], [1, 2048]]),
                       AP(ARNB_t, MM_XD, [[2 * ARN_COLS, 16], [1, 2048]]))
                dma_st(AP(mm_bc.tensor, 16 * 2048, [[2048, 16], [1, 2048]]),
                       AP(ARNB_t, MM_XD + 32 * 2 * ARN_COLS, [[2 * ARN_COLS, 16], [1, 2048]]))
                P.barrier()

                for th in range(2):
                    tb = th * 1024
                    for n in range(16):
                        dma(arn_b(MM_B + n * 1024, 1024),
                            AP(mm_bc.tensor, n * 2048 + tb, [[0, 128], [1, 1024]]), semn="lbc")
                        cbm = dma(arn_b(MM_C + n * 1024, 1024),
                                  AP(mm_bc.tensor, 16 * 2048 + n * 2048 + tb, [[0, 128], [1, 1024]]),
                                  semn="lbc")
                    P.wait("vector", "lbc", cbm)
                    zpre = {}
                    gate_v = {}
                    for c in range(16):
                        if c >= 2:
                            P.wait("sync", V, gate_v[c - 2])
                        zpre[c] = dma(AP(SC4B_t, (c % 2) * 1024, [[4096, 128], [1, 1024]]),
                                      AP(mm_z.tensor, c * 128 * 2048 + tb, [[2048, 128], [1, 1024]]),
                                      semn="lz" + str(c % 2))
                        if c >= 1:
                            P.wait("tensor", SS, sp_m)
                        P.wait("tensor", LG, cw_dtw)
                        for q in range(2):
                            P.op("tensor", lambda e, q=q, c=c: e.matmul(
                                PS[2 + q][:, :],
                                AP(wsl.ap().tensor, 64 * 4096 + c * 128, [[4096, 64], [1, 128]]),
                                AP(ARNB_t, 64 * 2 * ARN_COLS + MM_XD + tb + q * 512,
                                   [[2 * ARN_COLS, 64], [1, 512]]),
                                start=True, stop=True), inc=(T if q == 1 else None))
                        dmm = P.c(T)
                        P.wait("scalar", T, dmm)
                        if c >= 1:
                            P.wait("scalar", V, du_m)
                        for q in range(2):
                            P.op("scalar", lambda e, q=q, c=c: e.activation(
                                out=sl(sc3, q * 512, 128, 512, 2048), in_=PS[2 + q][:, :],
                                func=AF.Relu, bias=sl(sc1, 336 + c, 128, 1, 2048)))
                        for q in range(2):
                            P.op("scalar", lambda e, q=q, c=c: e.activation(
                                out=sl(sc3, 1024 + q * 512, 128, 512, 2048), in_=PS[2 + q][:, :],
                                func=AF.Abs, bias=sl(sc1, 336 + c, 128, 1, 2048)),
                                inc=(SS if q == 1 else None))
                        sp_m = P.c(SS)
                        P.op("scalar", lambda e: e.activation(
                            out=sl(sc2, 1024, 128, 1024, 2048), in_=sl(sc3, 1024, 128, 1024, 2048),
                            func=AF.Exp, scale=-1.0))
                        P.op("scalar", lambda e: e.activation(
                            out=sl(sc2, 1024, 128, 1024, 2048), in_=sl(sc2, 1024, 128, 1024, 2048),
                            func=AF.Ln, bias=1.0), inc=SS)
                        P.wait("vector", SS, P.c(SS))
                        P.op("vector", lambda e: e.tensor_add(
                            sl(sc2, 0, 128, 1024, 2048), sl(sc3, 0, 128, 1024, 2048),
                            sl(sc2, 1024, 128, 1024, 2048)), inc=V)
                        P.wait("vector", V, P.c(V))
                        P.op("vector", lambda e, c=c, tb=tb: e.tensor_mul(
                            AP(SC2B_t, 2048, [[4096, 128], [1, 1024]]),
                            sl(sc2, 0, 128, 1024, 2048),
                            arn_b(MM_U + c * 2048 + tb, 1024)), inc=V)
                        du_m = P.c(V)
                        imm = 0
                        sc_mark = {}
                        for n in range(16):
                            ab = (sl(sc3, 0, 128, 1024, 2048) if n % 2 == 0
                                  else sl(sc4, 1024, 128, 1024, 2048))
                            if n < 1:
                                P.wait("scalar", V, du_m)
                            else:
                                P.wait("scalar", V, sc_mark[n - 1])
                            am = P.op("scalar", lambda e, c=c, n=n, ab=ab: e.activation(
                                out=ab, in_=sl(sc2, 0, 128, 1024, 2048),
                                func=AF.Exp, scale=sl(sc1, c * 16 + n, 128, 1, 2048)), inc=SS)
                            if n > 0:
                                P.wait("vector", T, imm)
                            P.op("vector", lambda e, n=n: e.tensor_tensor(
                                AP(SC2B_t, 3072, [[4096, 128], [1, 1024]]),
                                AP(SC2B_t, 2048, [[4096, 128], [1, 1024]]),
                                arn_b(MM_B + n * 1024, 1024), ALU.mult), inc=V)
                            P.wait("vector", SS, am)
                            init = 0.0 if th == 0 else sl(sc1, 368 + c * 16 + n, 128, 1, 2048)
                            P.op("vector", lambda e, init=init, ab=ab: e.tensor_tensor_scan(
                                out=AP(SC3B_t, 2048, [[4096, 128], [1, 1024]]),
                                data0=ab,
                                data1=AP(SC2B_t, 3072, [[4096, 128], [1, 1024]]),
                                initial=init, op0=ALU.mult, op1=ALU.add), inc=V)
                            sc_mark[n] = P.c(V)
                            hco = (n % 2) * 1024
                            P.op("vector", lambda e, hco=hco, n=n: e.tensor_tensor(
                                AP(BTRB_t, hco, [[2048, 128], [1, 1024]]),
                                AP(SC3B_t, 2048, [[4096, 128], [1, 1024]]),
                                arn_b(MM_C + n * 1024, 1024), ALU.mult), inc=V)
                            hcm = P.c(V)
                            if th == 0:
                                P.op("vector", lambda e, c=c, n=n: e.tensor_copy(
                                    sl(sc1, 368 + c * 16 + n, 128, 1, 2048),
                                    AP(SC3B_t, 2048 + 1023, [[4096, 128], [1, 1]])), inc=V)
                            P.wait("tensor", V, hcm)
                            for q in range(2):
                                P.op("tensor", lambda e, q=q, n=n, hco=hco: e.matmul(
                                    PS[q][:, :], sl(wsm, 0, 128, 128, 1024),
                                    AP(BTRB_t, hco + q * 512, [[2048, 128], [1, 512]]),
                                    start=(n == 0), stop=(n == 15)), inc=(T if q == 1 else None))
                            imm = P.c(T)
                        P.wait("vector", T, imm)
                        P.wait("vector", "lz" + str(c % 2), zpre[c])
                        for q in range(2):
                            P.op("vector", lambda e, c=c, q=q, tb=tb: e.scalar_tensor_tensor(
                                sl(sc2, 1024 + q * 512, 128, 512, 2048),
                                arn_b(MM_U + c * 2048 + tb + q * 512, 512),
                                sl(sc1, 352 + c, 128, 1, 2048),
                                PS[q][:, :], ALU.mult, ALU.add), inc=V)
                        P.wait("vector", V, P.c(V))
                        for q in range(2):
                            P.op("vector", lambda e, c=c, q=q, tb=tb: e.tensor_mul(
                                arn_b(MM_U + c * 2048 + tb + q * 512, 512),
                                sl(sc2, 1024 + q * 512, 128, 512, 2048),
                                AP(SC4B_t, (c % 2) * 1024 + q * 512, [[4096, 128], [1, 512]])),
                                inc=(V if q == 1 else None))
                        du_m = P.c(V)
                        gate_v[c] = du_m
                    P.barrier()

                P.wait("gpsimd", T, P.c(T))
                dmark = {}
                vmark = {}
                stm = {}
                for dc in range(8):
                    buf = dc % 2
                    if dc >= 2:
                        P.wait("gpsimd", T, dmark[dc - 2])
                    sgn = "lgA" if buf == 0 else "lgB"
                    cw = dma_cast(sl(wsl, buf * 2048, 128, 2048, 4096),
                                  AP(moWB.tensor, (dr * 8 + dc) * 128 * 2048,
                                     [[2048, 128], [1, 2048]]), semn=sgn)
                    P.wait("tensor", sgn, cw)
                    for tq in range(4):
                        t0 = tq * 512
                        pb = 4 + (tq % 2)
                        if dr == 1:
                            c_x = dma(sl(sc3, 512 + (tq % 2) * 512, 128, 512, 2048),
                                      AP(x_cur.tensor, (b * D + dc * 128) * S + t0,
                                         [[S, 128], [1, 512]]), semn="lx" + str(tq % 2))
                            c_yp = dma(sl(sc2, (tq % 2) * 512, 128, 512, 2048),
                                       AP(mm_yp.tensor, dc * 128 * S + t0, [[S, 128], [1, 512]]),
                                       semn="lx" + str(tq % 2))
                        kk = (dc, tq)
                        if (dc, tq - 2) in vmark:
                            P.wait("tensor", SS, P.c(SS))
                            P.wait("tensor", V, vmark[(dc, tq - 2)])
                        elif dc >= 1 and (dc - 1, tq + 2) in vmark:
                            P.wait("tensor", SS, P.c(SS))
                            P.wait("tensor", V, vmark[(dc - 1, tq + 2)])
                        for c in range(16):
                            if dr == 0:
                                rap = arn_b(MM_U + c * 2048 + t0, 512)
                            else:
                                rap = AP(ARNB_t, MM_U + c * 2048 + (2047 - t0),
                                         [[2 * ARN_COLS, 128], [-1, 512]])
                            P.op("tensor", lambda e, c=c, buf=buf, rap=rap, pb=pb: e.matmul(
                                PS[pb][:, :], sl(wsl, buf * 2048 + c * 128, 128, 128, 4096),
                                rap, start=(c == 0), stop=(c == 15)),
                                inc=(T if c == 15 else None))
                        mm_done = P.c(T)
                        if dr == 0:
                            P.wait("scalar", T, mm_done)
                            if (dc, tq - 2) in stm:
                                P.wait("scalar", STQ, stm[(dc, tq - 2)])
                            elif dc >= 1 and (dc - 1, tq + 2) in stm:
                                P.wait("scalar", STQ, stm[(dc - 1, tq + 2)])
                            P.op("scalar", lambda e, tq=tq, pb=pb: e.activation(
                                out=sl(sc3, (tq % 2) * 512, 128, 512, 2048),
                                in_=PS[pb][:, :], func=AF.Copy), inc=SS)
                            vmark[kk] = 0
                            P.wait("sync", SS, P.c(SS))
                            stm[kk] = dma_st(
                                AP(mm_yp.tensor, dc * 128 * S + t0, [[S, 128], [1, 512]]),
                                sl(sc3, (tq % 2) * 512, 128, 512, 2048))
                        else:
                            P.wait("vector", T, mm_done)
                            P.wait("vector", "lx" + str(tq % 2), c_yp)
                            P.op("vector", lambda e, tq=tq, pb=pb: e.tensor_add(
                                sl(sc2, 1024 + (tq % 2) * 512, 128, 512, 2048),
                                PS[pb][:, :], sl(sc2, (tq % 2) * 512, 128, 512, 2048)))
                            P.wait("vector", "lx" + str(tq % 2), c_x)
                            if (dc, tq - 2) in stm:
                                P.wait("vector", STQ, stm[(dc, tq - 2)])
                            elif dc >= 1 and (dc - 1, tq + 2) in stm:
                                P.wait("vector", STQ, stm[(dc - 1, tq + 2)])
                            P.op("vector", lambda e, tq=tq: e.scalar_tensor_tensor(
                                sl(sc3, 1024 + (tq % 2) * 512, 128, 512, 2048),
                                sl(sc2, 1024 + (tq % 2) * 512, 128, 512, 2048), 0.5,
                                sl(sc3, 512 + (tq % 2) * 512, 128, 512, 2048),
                                ALU.mult, ALU.add), inc=V)
                            vmark[kk] = P.c(V)
                            P.wait("sync", V, vmark[kk])
                            stm[kk] = dma_st(
                                AP(x_cur.tensor, (b * D + dc * 128) * S + t0, [[S, 128], [1, 512]]),
                                sl(sc3, 1024 + (tq % 2) * 512, 128, 512, 2048))
                    dmark[dc] = P.c(T)
                P.barrier()

        for b in range(BL):
            emit_attn(b)
        if N_SUBLAYERS >= 2:
            for b in range(BL):
                emit_ffn(b)
        if N_SUBLAYERS >= 3:
            for b in range(BL):
                emit_mamba(b)
        if N_SUBLAYERS >= 4:
            for b in range(BL):
                emit_ffn(b)

        P.barrier()
        c_out = dma_st(out_xT, x_cur)
        P.wait("sync", STQ, P.c(STQ))

        P.replay(block)
    return nc



def _prep_shared(i):
    f32 = np.float32
    o = {}
    o["rpe"] = np.ascontiguousarray(i["L0_attn_a_rpe_bias"], f32)
    qkvT = np.asarray(i["L0_attn_a_qkv"], f32).reshape(D, 3 * NH * DH)
    o["qkvT"] = np.ascontiguousarray(qkvT)
    o["qkvB"] = np.ascontiguousarray(qkvT.reshape(8, 128, 24, 128).transpose(2, 1, 0, 3).reshape(24, 128, 1024))
    o["outwT"] = np.ascontiguousarray(np.asarray(i["L0_attn_a_out_w"], f32).reshape(NH * DH, D))

    def mamba_dir(sfx):
        d = {}
        inWT = np.asarray(i[f"L0_attn_b_inW_{sfx}"], f32).T
        d["inWB"] = np.ascontiguousarray(inWT.reshape(8, 128, 32, 128).transpose(2, 1, 0, 3).reshape(32, 128, 1024))
        d["convWB"] = np.ascontiguousarray(np.asarray(i[f"L0_attn_b_convW_{sfx}"], f32)[:, 0, :].reshape(16, 128, 4).transpose(1, 0, 2).reshape(128, 64))
        d["convBv"] = np.ascontiguousarray(np.asarray(i[f"L0_attn_b_convB_{sfx}"], f32).reshape(16, 128).T)
        xp = np.asarray(i[f"L0_attn_b_xpW_{sfx}"], f32)
        pad = np.zeros((DI, 128), f32)
        pad[:, 0:16] = xp[64:80].T
        pad[:, 32:48] = xp[80:96].T
        pad[:, 64:128] = xp[0:64].T
        d["xpWB"] = np.ascontiguousarray(pad.reshape(16, 128, 128).transpose(1, 0, 2).reshape(128, 2048))
        d["dtWT"] = np.ascontiguousarray(np.asarray(i[f"L0_attn_b_dtW_{sfx}"], f32).T)
        d["dtBv"] = np.ascontiguousarray(np.asarray(i[f"L0_attn_b_dtB_{sfx}"], f32).reshape(16, 128).T)
        d["AlogB"] = np.ascontiguousarray(np.asarray(i[f"L0_attn_b_Alog_{sfx}"], f32).reshape(16, 128, 16).transpose(1, 0, 2).reshape(128, 256))
        d["Dpv"] = np.ascontiguousarray(np.asarray(i[f"L0_attn_b_Dp_{sfx}"], f32).reshape(16, 128).T)
        moWT = np.asarray(i[f"L0_attn_b_outW_{sfx}"], f32).T
        d["moWB"] = np.ascontiguousarray(moWT.reshape(16, 128, 8, 128).transpose(2, 1, 0, 3).reshape(8, 128, 2048))
        return d

    mf, mb = mamba_dir("f"), mamba_dir("b")
    for k in mf:
        o[k] = np.ascontiguousarray(np.stack([mf[k], mb[k]], axis=0))

    o["gateB"] = np.ascontiguousarray(np.asarray(i["L0_ffn_gate_w"], f32).T.reshape(8, 128, 8).transpose(1, 0, 2).reshape(128, 64))
    WgT = np.asarray(i["L0_ffn_w_gate"], f32).T
    WuT = np.asarray(i["L0_ffn_w_up"], f32).T
    WdT = np.asarray(i["L0_ffn_w_down"], f32).T
    o["WgB"] = np.ascontiguousarray(WgT.reshape(8, 128, 32, 128).transpose(2, 1, 0, 3).reshape(32, 128, 1024))
    o["WuB"] = np.ascontiguousarray(WuT.reshape(8, 128, 32, 128).transpose(2, 1, 0, 3).reshape(32, 128, 1024))
    o["WdB"] = np.ascontiguousarray(WdT.reshape(32, 128, 8, 128).transpose(2, 1, 0, 3).reshape(8, 128, 4096))
    o["A1cB"] = np.ascontiguousarray(np.asarray(i["L0_ffn_A1"], f32).reshape(128, D).T.reshape(8, 128, 128).transpose(1, 0, 2).reshape(128, 1024))
    o["A3cB"] = np.ascontiguousarray(np.asarray(i["L0_ffn_A3"], f32).reshape(128, D).T.reshape(8, 128, 128).transpose(1, 0, 2).reshape(128, 1024))
    o["B1c"] = np.ascontiguousarray(np.asarray(i["L0_ffn_B1"], f32).transpose(0, 2, 1).reshape(128, F))
    o["B3c"] = np.ascontiguousarray(np.asarray(i["L0_ffn_B3"], f32).transpose(0, 2, 1).reshape(128, F))
    o["A2cB"] = np.ascontiguousarray(np.asarray(i["L0_ffn_A2"], f32).transpose(2, 0, 1).reshape(32, 128, 128).transpose(1, 0, 2).reshape(128, 4096))
    o["B2c"] = np.ascontiguousarray(np.asarray(i["L0_ffn_B2"], f32).transpose(0, 2, 1).reshape(128, D))

    o["Jrev"] = np.eye(128, dtype=f32)[::-1].copy()
    o["Ident"] = np.eye(128, dtype=f32)
    o["Itwo"] = np.eye(128, dtype=f32) * 2.0
    o["onesc"] = np.ones((128, 1), f32)
    o["ones1r"] = np.ones((1, 128), f32)
    ex = np.zeros((E, 128), f32)
    for e in range(E):
        ex[e, e * 16:(e + 1) * 16] = 2.0
    o["exp16"] = ex
    cx = np.zeros((128, 8), f32)
    cx[:, 0] = EPS
    cx[:, 1] = 1.0 / D
    cx[:, 2] = -0.5
    cx[:, 3] = 0.125
    o["cextra"] = cx
    return o


def kernel(**inputs):
    shared = _prep_shared(inputs)
    x = np.asarray(inputs["L0_attn_a_x"], np.float32)
    nc = build_program()
    in_maps = []
    for c in range(8):
        m = dict(shared)
        xs = x[2 * c:2 * c + 2]
        m["xT"] = np.ascontiguousarray(xs.transpose(0, 2, 1))
        in_maps.append(m)
    res = run_bass_kernel_spmd(nc, in_maps, core_ids=list(range(8)))
    out = np.empty((16, S, D), np.float32)
    for c in range(8):
        o = np.asarray(res.results[c]["out_xT"])
        out[2 * c:2 * c + 2] = o.transpose(0, 2, 1)
    return out
